# version 04
# baseline (speedup 1.0000x reference)
import numpy as np
import ml_dtypes
import concourse.bass as bass
import concourse.mybir as mybir
from concourse.bass_utils import run_bass_kernel_spmd
from concourse.aot_env import allow_ici_all_to_all
from contextlib import ExitStack

F32 = mybir.dt.float32
BF16 = mybir.dt.bfloat16
AF = mybir.ActivationFunctionType
ALU = mybir.AluOpType
BF = ml_dtypes.bfloat16

NCORES = 8
P = 128
L = 8192
D = 2048
HF = 8192
DM = 4096
NS = 16
RDT = 128
TPC = L // NCORES
DSC = DM // NCORES
NCH_D = D // P
NCH_H = HF // P
NDT = DSC // P
TSC = 512
NTCH = L // TSC
INV_SQRT_N = float(1.0 / np.sqrt(HF))

FENCE_SEMS = ["t_ms", "v_ms", "s_ms", "g_ms"]
_STOP_PHASE = 99
_DEBUG_TAPS = False


def _had(n):
    h = np.array([[1.0]], dtype=np.float32)
    while h.shape[0] < n:
        h = np.block([[h, h], [h, -h]])
    return h


class Ctx:
    def __init__(self, nc, es):
        self.nc = nc
        self.es = es
        self.h = {}
        self.n = {}

    def sem(self, name):
        if name not in self.h:
            self.h[name] = self.es.enter_context(self.nc.semaphore(name))
            self.n[name] = 0
        return self.h[name]

    def cnt(self, name):
        self.sem(name)
        return self.n[name]

    def inc(self, instr, name, amt=1):
        instr.then_inc(self.sem(name), amt)
        self.n[name] += amt
        return self.n[name]

    def dma(self, eng, out, in_, name):
        return self.inc(eng.dma_start(out=out, in_=in_), name, 16)

    def wge(self, eng, name, val):
        if val > 0:
            eng.wait_ge(self.sem(name), val)

    def snap(self):
        return dict(self.n)

    def fence_wait(self, eng, snap):
        for k, v in snap.items():
            if v > 0:
                eng.wait_ge(self.sem(k), v)

    def expect(self, name, val):
        assert self.n[name] == val, f"sem {name}: built {self.n[name]} != predicted {val}"


def build_program():
    nc = bass.Bass("TRN2", target_bir_lowering=False)
    dr = {}

    def din(name, shape, dt=F32):
        dr[name] = nc.dram_tensor(name, list(shape), dt, kind="ExternalInput")
        return dr[name]

    din("x_t", (TPC, D))
    din("w1s", (NCH_H, P, NCH_D * P))
    din("w3s", (NCH_H, P, NCH_D * P))
    din("w2g", (HF, D))
    din("wouts", (4, 32, P, 512))
    din("wins", (8, P, NCH_D * P))
    din("wxs", (P, NDT * 160))
    din("wdts", (RDT, DSC))
    din("cws", (P, NDT * 4))
    din("cbs", (P, NDT))
    din("bdts", (P, NDT))
    din("alogs", (P, NDT * NS))
    din("dpars", (P, NDT))
    din("h128", (P, P), BF16)
    din("bd64", (P, P), BF16)
    din("eyef", (P, P))
    din("eyeb", (P, P), BF16)
    din("onesr", (1, P), BF16)
    din("onesc", (P, 1))
    din("esel", (32, 32 * P), BF16)

    out_t = nc.dram_tensor("out_t", [TPC, D], F32, kind="ExternalOutput")
    if _DEBUG_TAPS:
        dbg = {nm: nc.dram_tensor("dbg_" + nm, shp, dt, kind="ExternalOutput")
               for nm, shp, dt in [
                   ("ag", [NCORES * D, TPC], BF16), ("ar", [160, L], F32),
                   ("a2a", [NCORES * DSC, TPC], BF16), ("usp", [DSC, L], BF16),
                   ("gsp", [DSC, L], BF16), ("w2p", [HF, D], BF16),
                   ("s1", [NCH_H, P, TPC], BF16), ("ari", [160, L], F32)]}

    w1bf = nc.dram_tensor("w1bf", [NCH_H, P, NCH_D * P], BF16)
    w3bf = nc.dram_tensor("w3bf", [NCH_H, P, NCH_D * P], BF16)
    w2pbf = nc.dram_tensor("w2pbf", [HF, D], BF16)
    s1sp = nc.dram_tensor("s1sp", [NCH_H, P, TPC], BF16)
    usp = nc.dram_tensor("usp", [DSC, L], BF16)
    gsp = nc.dram_tensor("gsp", [DSC, L], BF16)
    ag_in = nc.dram_tensor("ag_in", [D, TPC], BF16)
    ag_out = nc.dram_tensor("ag_out", [NCORES * D, TPC], BF16)
    ar_in = nc.dram_tensor("ar_in", [160, L], F32)
    ar_out = nc.dram_tensor("ar_out", [160, L], F32)
    a2a_in = nc.dram_tensor("a2a_in", [NCORES * DSC, TPC], BF16)
    a2a_out = nc.dram_tensor("a2a_out", [NCORES * DSC, TPC], BF16)

    es = ExitStack()
    with es:
        c = Ctx(nc, es)
        E = es.enter_context

        sR = E(nc.sbuf_tensor("sR", [P, NCH_D * TPC], F32))
        sWf = [E(nc.sbuf_tensor(f"sWf{i}", [P, 2048], F32)) for i in range(2)]
        sWb = [E(nc.sbuf_tensor(f"sWb{i}", [P, 2048], BF16)) for i in range(2)]
        sH128 = E(nc.sbuf_tensor("sH128", [P, P], BF16))
        sBD64 = E(nc.sbuf_tensor("sBD64", [P, P], BF16))
        sEyeF = E(nc.sbuf_tensor("sEyeF", [P, P], F32))
        sEyeB = E(nc.sbuf_tensor("sEyeB", [P, P], BF16))
        sOnesR = E(nc.sbuf_tensor("sOnesR", [1, P], BF16))
        sOnesC = E(nc.sbuf_tensor("sOnesC", [P, 1], F32))
        sEsel = E(nc.sbuf_tensor("sEsel", [32, 32 * P], BF16))
        sA = E(nc.sbuf_tensor("sA", [P, NDT * NS], F32))
        sAtmp = E(nc.sbuf_tensor("sAtmp", [P, NDT * NS], F32))
        sCw = E(nc.sbuf_tensor("sCw", [P, NDT * 4], F32))
        sCb = E(nc.sbuf_tensor("sCb", [P, NDT], F32))
        sBdt = E(nc.sbuf_tensor("sBdt", [P, NDT], F32))
        sDp = E(nc.sbuf_tensor("sDp", [P, NDT], F32))
        sState = E(nc.sbuf_tensor("sState", [P, NDT * NS], F32))
        sEps = E(nc.sbuf_tensor("sEps", [P, 1], F32))
        sWdtb = E(nc.sbuf_tensor("sWdtb", [P, DSC], BF16))
        sWxb = E(nc.sbuf_tensor("sWxb", [P, NDT * 160], BF16))
        es0w = ExitStack()
        sWdtS = es0w.enter_context(nc.sbuf_tensor("sWdtS", [P, DSC], F32))
        sWxS = es0w.enter_context(nc.sbuf_tensor("sWxS", [P, NDT * 160], F32))

        block = E(nc.Block())

        c.dbg_sb = []

        def _emit_taps():
            if not _DEBUG_TAPS:
                return
            sems = list(c.h.items())
            counts = {k: c.n[k] for k, _ in sems}

            @block.gpsimd
            def _(g):
                for k, v in counts.items():
                    if v > 0 and not k.startswith("c"):
                        g.wait_ge(c.h[k], v)
                for nm, scr in [("ag", ag_out), ("ar", ar_out), ("a2a", a2a_out),
                                ("usp", usp), ("gsp", gsp), ("w2p", w2pbf), ("s1", s1sp),
                                ("ari", ar_in)]:
                    c.dma(g, dbg[nm].ap(), scr.ap(), "g_dbg")
                for nm, hnd, shp, dt in c.dbg_sb:
                    od = nc.dram_tensor("dbgsb_" + nm, list(shp), dt, kind="ExternalOutput")
                    c.dma(g, od.ap(), hnd[:], "g_dbg")
                c.wge(g, "g_dbg", c.cnt("g_dbg"))

        @block.sync
        def _(s):
            c.dma(s, sH128[:], dr["h128"][:, :], "q_ld")
            c.dma(s, sBD64[:], dr["bd64"][:, :], "q_ld")
            c.dma(s, sEyeF[:], dr["eyef"][:, :], "q_ld")
            c.dma(s, sEyeB[:], dr["eyeb"][:, :], "q_ld")
            c.dma(s, sOnesR[:], dr["onesr"][:, :], "q_ld")
            c.dma(s, sOnesC[:], dr["onesc"][:, :], "q_ld")
            c.dma(s, sEsel[:], dr["esel"][:, :], "q_ld")
            c.dma(s, sAtmp[:], dr["alogs"][:, :], "q_ld")
            c.dma(s, sCw[:], dr["cws"][:, :], "q_ld")
            c.dma(s, sCb[:], dr["cbs"][:, :], "q_ld")
            c.dma(s, sBdt[:], dr["bdts"][:, :], "q_ld")
            c.dma(s, sDp[:], dr["dpars"][:, :], "q_ld")
            c.dma(s, sWdtS[:], dr["wdts"][:, :], "q_ld")
            c.dma(s, sWxS[:], dr["wxs"][:, :], "q_ld")
        ld_consts = c.cnt("q_ld")

        @block.scalar
        def _(sc):
            c.wge(sc, "q_ld", ld_consts)
            c.inc(sc.activation(out=sA[:], in_=sAtmp[:], func=AF.Exp), "s_ms")
        s_exp = c.cnt("s_ms")

        @block.vector
        def _(v):
            c.wge(v, "s_ms", s_exp)
            v.tensor_scalar(out=sA[:], in0=sA[:], scalar1=-1.0, scalar2=None, op0=ALU.mult)
            c.wge(v, "q_ld", ld_consts)
            v.tensor_copy(out=sWdtb[:], in_=sWdtS[:])
            v.tensor_copy(out=sWxb[:], in_=sWxS[:])
            v.memset(sEps[:], 1e-5)
            c.inc(v.memset(sState[:], 0.0), "v_ms")
        es0w.close()

        if _STOP_PHASE < 1:
            _emit_taps()
            return nc
        with ExitStack() as es0:
            sXin = [es0.enter_context(nc.sbuf_tensor(f"sXin{i}", [P, D], F32)) for i in range(2)]
            psT = es0.enter_context(nc.psum_tensor("psT0", [P, D], F32))

            fence0b = c.snap()

            @block.sync
            def _(s):
                c.fence_wait(s, fence0b)
                for tt in range(8):
                    c.wge(s, "t_x", tt - 1)
                    c.dma(s, sXin[tt % 2][:], dr["x_t"][tt * P:(tt + 1) * P, :], "l_x")

            @block.tensor
            def _(t):
                c.wge(t, "q_ld", ld_consts)
                for tt in range(8):
                    c.wge(t, "l_x", 16 * (tt + 1))
                    c.wge(t, "v_x", tt)
                    for ch in range(NCH_D):
                        mm = t.transpose(psT[:, ch * P:(ch + 1) * P],
                                         sXin[tt % 2][:, ch * P:(ch + 1) * P], sEyeF[:])
                    c.inc(mm, "t_x")

            @block.vector
            def _(v):
                for tt in range(8):
                    c.wge(v, "t_x", tt + 1)
                    dst = sR[:].rearrange("p (ch t) -> p ch t", ch=NCH_D)[:, :, tt * P:(tt + 1) * P]
                    src = psT[:].rearrange("p (ch t) -> p ch t", ch=NCH_D)
                    c.inc(v.tensor_copy(out=dst, in_=src), "v_x")
                c.inc(v.engine_nop(), "v_ms")
            c.expect("t_x", 8)
            c.expect("v_x", 8)

        if _STOP_PHASE < 2:
            _emit_taps()
            return nc
        with ExitStack() as esp:
            sW2e = esp.enter_context(nc.sbuf_tensor("sW2e", [P, D], BF16))
            psW2 = esp.enter_context(nc.psum_tensor("psW2", [P, D], F32))
            NPP = 64
            fence0c = c.snap()

            @block.sync
            def _(s):
                c.fence_wait(s, fence0c)
                for pp in range(NPP):
                    c.wge(s, "g_wc", pp - 1)
                    c.dma(s, sWf[pp % 2][:], dr["w2g"][pp * P:(pp + 1) * P, :], "l_wf")

            def emit_w2dma(g, pp):
                c.wge(g, "s_w2", pp + 1)
                dst = w2pbf.ap().rearrange("(cc p) d -> p cc d", p=P)[pp * 2:pp * 2 + 2, :, :]
                c.dma(g, dst, sW2e[:], "g_w2")

            @block.gpsimd
            def _(g):
                for pp in range(NPP):
                    if pp >= 2:
                        emit_w2dma(g, pp - 2)
                    c.wge(g, "l_wf", 16 * (pp + 1))
                    c.wge(g, "t_w2", pp - 1)
                    c.inc(g.tensor_copy(out=sWb[pp % 2][:], in_=sWf[pp % 2][:]), "g_wc")
                emit_w2dma(g, NPP - 2)
                emit_w2dma(g, NPP - 1)
                c.inc(g.engine_nop(), "g_ms")

            @block.tensor
            def _(t):
                c.fence_wait(t, fence0c)
                for pp in range(NPP):
                    c.wge(t, "g_wc", pp + 1)
                    c.wge(t, "s_w2", pp)
                    for q in range(4):
                        mm = t.matmul(psW2[:, q * 512:(q + 1) * 512], sBD64[:],
                                      sWb[pp % 2][:, q * 512:(q + 1) * 512], start=True, stop=True)
                    c.inc(mm, "t_w2")
                c.inc(mm, "t_ms") if False else None

            @block.scalar
            def _(sc):
                c.fence_wait(sc, fence0c)
                for pp in range(NPP):
                    c.wge(sc, "t_w2", pp + 1)
                    c.wge(sc, "g_w2", 16 * pp)
                    c.inc(sc.activation(out=sW2e[:], in_=psW2[:, :], func=AF.Copy), "s_w2")

            c.expect("t_w2", NPP)
            c.expect("s_w2", NPP)
            c.expect("g_w2", 16 * NPP)

        def emit_ln(tag, sXn, fence):
            with ExitStack() as esl:
                sSq = esl.enter_context(nc.sbuf_tensor(f"sSq{tag}", [P, TPC], F32))
                sStat = esl.enter_context(nc.sbuf_tensor(f"sStat{tag}", [1, 4 * TPC], F32))
                sStatB = esl.enter_context(nc.sbuf_tensor(f"sStatB{tag}", [1, 2 * TPC], BF16))
                psSum = esl.enter_context(nc.psum_tensor(f"psSum{tag}", [1, TPC], F32))
                psSq = esl.enter_context(nc.psum_tensor(f"psSq{tag}", [1, TPC], F32))
                psBrd = esl.enter_context(nc.psum_tensor(f"psBrd{tag}", [P, 2 * TPC], F32))
                b_tq = c.cnt("t_lnq")
                b_ts = c.cnt("t_lns")
                b_ss = c.cnt("s_lnq")
                b_vs = c.cnt("v_lns")
                b_sd = c.cnt("s_lnd")
                b_tb = c.cnt("t_lnb")
                b_vl = c.cnt("v_ln")

                MU = sStat[0:1, 0:TPC]
                MSQ = sStat[0:1, TPC:2 * TPC]
                MU2 = sStat[0:1, 2 * TPC:3 * TPC]
                VAR = sStat[0:1, TPC:2 * TPC]
                SD = sStat[0:1, TPC:2 * TPC]
                RSTD = sStat[0:1, TPC:2 * TPC]
                BNEG = sStat[0:1, 3 * TPC:4 * TPC]

                @block.scalar
                def _(sc):
                    c.fence_wait(sc, fence)
                    for ch in range(NCH_D):
                        c.wge(sc, "t_lnq", b_tq + ch)
                        c.inc(sc.activation(out=sSq[:], in_=sR[:, ch * TPC:(ch + 1) * TPC],
                                            func=AF.Square), "s_lnq")

                @block.tensor
                def _(t):
                    c.fence_wait(t, fence)
                    for ch in range(NCH_D):
                        for hh in range(2):
                            mm = t.matmul(psSum[:, hh * 512:(hh + 1) * 512], sOnesC[:],
                                          sR[:, ch * TPC + hh * 512: ch * TPC + (hh + 1) * 512],
                                          start=(ch == 0), stop=(ch == NCH_D - 1))
                    c.inc(mm, "t_lns")
                    for ch in range(NCH_D):
                        c.wge(t, "s_lnq", b_ss + ch + 1)
                        for hh in range(2):
                            mm = t.matmul(psSq[:, hh * 512:(hh + 1) * 512], sOnesC[:],
                                          sSq[:, hh * 512:(hh + 1) * 512],
                                          start=(ch == 0), stop=(ch == NCH_D - 1))
                        c.inc(mm, "t_lnq")

                @block.vector
                def _(v):
                    c.fence_wait(v, fence)
                    c.wge(v, "t_lns", b_ts + 1)
                    c.wge(v, "t_lnq", b_tq + NCH_D)
                    v.tensor_scalar(out=MU, in0=psSum[:, :], scalar1=1.0 / D, scalar2=None, op0=ALU.mult)
                    v.tensor_scalar(out=MSQ, in0=psSq[:, :], scalar1=1.0 / D, scalar2=None, op0=ALU.mult)
                    v.tensor_tensor(out=MU2, in0=MU, in1=MU, op=ALU.mult)
                    c.inc(v.tensor_tensor(out=VAR, in0=MSQ, in1=MU2, op=ALU.subtract), "v_lns")

                @block.scalar
                def _(sc):
                    c.wge(sc, "v_lns", b_vs + 1)
                    c.inc(sc.activation(out=SD, in_=VAR, func=AF.Sqrt, bias=sEps[0:1, :]), "s_lnd")

                @block.vector
                def _(v):
                    c.wge(v, "s_lnd", b_sd + 1)
                    v.reciprocal(out=RSTD, in_=SD)
                    v.scalar_tensor_tensor(out=BNEG, in0=MU, scalar=-1.0, in1=RSTD,
                                           op0=ALU.mult, op1=ALU.mult)
                    v.tensor_copy(out=sStatB[0:1, 0:TPC], in_=RSTD)
                    c.inc(v.tensor_copy(out=sStatB[0:1, TPC:2 * TPC], in_=BNEG), "v_lns")

                @block.tensor
                def _(t):
                    c.wge(t, "v_lns", b_vs + 2)
                    for q in range(4):
                        mm = t.matmul(psBrd[:, q * 512:(q + 1) * 512], sOnesR[:],
                                      sStatB[0:1, q * 512:(q + 1) * 512], start=True, stop=True)
                    c.inc(mm, "t_lnb")

                @block.vector
                def _(v):
                    c.wge(v, "t_lnb", b_tb + 1)
                    for ch in range(NCH_D):
                        v.tensor_tensor(out=sSq[:], in0=sR[:, ch * TPC:(ch + 1) * TPC],
                                        in1=psBrd[:, 0:TPC], op=ALU.mult)
                        last = v.tensor_tensor(out=sXn[:, ch * TPC:(ch + 1) * TPC], in0=sSq[:],
                                               in1=psBrd[:, TPC:2 * TPC], op=ALU.add)
                    c.inc(last, "v_ln")
                    c.inc(v.engine_nop(), "v_ms")
                c.expect("v_ln", b_vl + 1)

        def emit_ffn_a(first, sXn, fence):
            wsrc1 = dr["w1s"] if first else w1bf
            wsrc3 = dr["w3s"] if first else w3bf
            with ExitStack() as esf:
                tg = "a" if first else "b"
                sUGs = esf.enter_context(nc.sbuf_tensor("sUGs" + tg, [P, TPC], BF16))
                sHc = esf.enter_context(nc.sbuf_tensor("sHc" + tg, [P, TPC], BF16))
                sS1e = esf.enter_context(nc.sbuf_tensor("sS1e" + tg, [P, TPC], BF16))
                psA = esf.enter_context(nc.psum_tensor("psAa" if first else "psAb", [P, TPC], F32))
                psB = esf.enter_context(nc.psum_tensor("psBa" if first else "psBb", [P, TPC], F32))
                psC = esf.enter_context(nc.psum_tensor("psCa" if first else "psCb", [P, TPC], F32))
                b_lw = c.cnt("l_wf")
                b_gc = c.cnt("g_wc")
                b_ta = c.cnt("t_fa")
                b_ss = c.cnt("s_fsi")
                b_vm = c.cnt("v_fmu")
                b_t1 = c.cnt("t_fs1")
                b_se = c.cnt("s_fev")
                b_gs = c.cnt("g_fs1")
                b_gw = c.cnt("g_fwc")
                NU = 2 * NCH_H

                @block.sync
                def _(s):
                    c.fence_wait(s, fence)
                    for u in range(NU):
                        m, hc = u % 2, u // 2
                        if first:
                            c.wge(s, "g_wc", b_gc + u - 1)
                            c.dma(s, sWf[u % 2][:], (wsrc1 if m == 0 else wsrc3)[hc, :, :], "l_wf")
                        else:
                            c.wge(s, "t_fa", b_ta + u - 1)
                            c.dma(s, sWb[u % 2][:], (wsrc1 if m == 0 else wsrc3)[hc, :, :], "l_wf")

                def emit_spill(g, hc):
                    c.wge(g, "s_fev", b_se + hc + 1)
                    c.dma(g, s1sp[hc, :, :], sS1e[:], "g_fs1")

                if first:
                    @block.gpsimd
                    def _(g):
                        c.fence_wait(g, fence)
                        for u in range(NU):
                            m, hc = u % 2, u // 2
                            if m == 1 and hc >= 1:
                                emit_spill(g, hc - 1)
                            c.wge(g, "l_wf", b_lw + 16 * (u + 1))
                            c.wge(g, "t_fa", b_ta + u - 1)
                            c.wge(g, "g_fwc", b_gw + 16 * (u - 1))
                            c.inc(g.tensor_copy(out=sWb[u % 2][:], in_=sWf[u % 2][:]), "g_wc")
                            c.dma(g, (w1bf if m == 0 else w3bf)[hc, :, :], sWb[u % 2][:], "g_fwc")
                        emit_spill(g, NCH_H - 1)
                        c.inc(g.engine_nop(), "g_ms")

                @block.tensor
                def _(t):
                    c.fence_wait(t, fence)
                    for hc in range(NCH_H):
                        for m in range(2):
                            u = 2 * hc + m
                            if first:
                                c.wge(t, "g_wc", b_gc + u + 1)
                            else:
                                c.wge(t, "l_wf", b_lw + 16 * (u + 1))
                            if m == 0:
                                c.wge(t, "s_fsi", b_ss + hc)
                            else:
                                c.wge(t, "v_fmu", b_vm + hc)
                            ps = psA if m == 0 else psB
                            for kc in range(NCH_D):
                                for q in range(2):
                                    mm = t.matmul(ps[:, q * 512:(q + 1) * 512],
                                                  sWb[u % 2][:, kc * P:(kc + 1) * P],
                                                  sXn[:, kc * TPC + q * 512: kc * TPC + (q + 1) * 512],
                                                  start=(kc == 0), stop=(kc == NCH_D - 1))
                            c.inc(mm, "t_fa")
                        c.wge(t, "v_fmu", b_vm + hc + 1)
                        c.wge(t, "s_fev", b_se + hc)
                        for q in range(2):
                            mm = t.matmul(psC[:, q * 512:(q + 1) * 512], sH128[:],
                                          sHc[:, q * 512:(q + 1) * 512], start=True, stop=True)
                        c.inc(mm, "t_fs1")

                @block.scalar
                def _(sc):
                    c.fence_wait(sc, fence)
                    for hc in range(NCH_H):
                        c.wge(sc, "t_fa", b_ta + 2 * hc + 1)
                        c.wge(sc, "v_fmu", b_vm + hc)
                        c.inc(sc.activation(out=sUGs[:], in_=psA[:, :], func=AF.Silu), "s_fsi")
                        c.wge(sc, "t_fs1", b_t1 + hc + 1)
                        c.wge(sc, "g_fs1", b_gs + 16 * hc)
                        c.inc(sc.activation(out=sS1e[:], in_=psC[:, :], func=AF.Copy,
                                            scale=INV_SQRT_N), "s_fev")

                @block.vector
                def _(v):
                    c.fence_wait(v, fence)
                    for hc in range(NCH_H):
                        c.wge(v, "s_fsi", b_ss + hc + 1)
                        c.wge(v, "t_fa", b_ta + 2 * hc + 2)
                        c.wge(v, "t_fs1", b_t1 + hc)
                        c.inc(v.tensor_tensor(out=sHc[:], in0=sUGs[:], in1=psB[:, :], op=ALU.mult),
                              "v_fmu")
                    c.inc(v.engine_nop(), "v_ms")

                if not first:
                    @block.gpsimd
                    def _(g):
                        for hc in range(NCH_H):
                            emit_spill(g, hc)
                        c.inc(g.engine_nop(), "g_ms")
                c.expect("t_fa", b_ta + NU)
                c.expect("t_fs1", b_t1 + NCH_H)
                c.expect("s_fsi", b_ss + NCH_H)
                c.expect("s_fev", b_se + NCH_H)
                c.expect("v_fmu", b_vm + NCH_H)
                c.expect("g_fs1", b_gs + 16 * NCH_H)
            return b_gs

        def emit_ffn_b(first, b_gs):
            with ExitStack() as esb:
                sS1tc = esb.enter_context(
                    nc.sbuf_tensor("sS1t" + ("a" if first else "b"), [P, NCH_H * 512], BF16))
                psO = [esb.enter_context(
                    nc.psum_tensor(("psOa" if first else "psOb") + str(j), [P, 512], F32))
                    for j in range(4)]
                b_l1 = c.cnt("l_s1")
                b_lw2 = c.cnt("l_w2")
                b_tb = c.cnt("t_fb")
                b_vb = c.cnt("v_fb")
                se_a_done = c.cnt("s_fev")
                vm_a_done = c.cnt("v_fmu")

                fence_b = c.snap()

                @block.sync
                def _(s):
                    c.fence_wait(s, fence_b)
                    for tc in range(2):
                        if tc == 1:
                            c.wge(s, "t_fb", b_tb + 256)
                        for hc in range(NCH_H):
                            c.wge(s, "g_fs1", b_gs + 16 * (hc + 1))
                            c.dma(s, sS1tc[:, hc * 512:(hc + 1) * 512],
                                  s1sp[hc, :, tc * 512:(tc + 1) * 512], "l_s1")
                        for k in range(256):
                            dg, cb = k // NCH_H, k % NCH_H
                            gk = tc * 256 + k
                            c.wge(s, "t_fb", b_tb + gk - 1)
                            c.dma(s, sWb[gk % 2][:, 0:512],
                                  w2pbf[cb * P:(cb + 1) * P, dg * 512:(dg + 1) * 512], "l_w2")

                @block.tensor
                def _(t):
                    c.wge(t, "s_fev", se_a_done)
                    c.wge(t, "v_fmu", vm_a_done)
                    for tc in range(2):
                        c.wge(t, "l_s1", b_l1 + 16 * NCH_H * (tc + 1))
                        for k in range(256):
                            dg, cb = k // NCH_H, k % NCH_H
                            gk = tc * 256 + k
                            c.wge(t, "l_w2", b_lw2 + 16 * (gk + 1))
                            if cb == 0:
                                c.wge(t, "v_fb", b_vb + tc * 4 + dg)
                            for j in range(4):
                                mm = t.matmul(psO[j][:, :], sWb[gk % 2][:, j * P:(j + 1) * P],
                                              sS1tc[:, cb * 512:(cb + 1) * 512],
                                              start=(cb == 0), stop=(cb == NCH_H - 1))
                            c.inc(mm, "t_fb")

                @block.vector
                def _(v):
                    for tc in range(2):
                        for dg in range(4):
                            c.wge(v, "t_fb", b_tb + tc * 256 + (dg + 1) * NCH_H)
                            for j in range(4):
                                dc = dg * 4 + j
                                sl = sR[:, dc * TPC + tc * 512: dc * TPC + (tc + 1) * 512]
                                last = v.scalar_tensor_tensor(out=sl, in0=psO[j][:, :], scalar=0.5,
                                                              in1=sl, op0=ALU.mult, op1=ALU.add)
                            c.inc(last, "v_fb")
                    c.inc(v.engine_nop(), "v_ms")
                c.expect("t_fb", b_tb + 512)
                c.expect("v_fb", b_vb + 8)

        if _STOP_PHASE < 3:
            _emit_taps()
            return nc
        fence = c.snap()
        with ExitStack() as esx:
            sXn = esx.enter_context(nc.sbuf_tensor("sXn1", [P, NCH_D * TPC], BF16))
            emit_ln("1", sXn, fence)
            fence = c.snap()
            ffn1_bgs = emit_ffn_a(True, sXn, fence)
        if _STOP_PHASE < 4:
            _emit_taps()
            return nc
        emit_ffn_b(True, ffn1_bgs)

        if _STOP_PHASE < 5:
            _emit_taps()
            return nc
        fence = c.snap()
        with ExitStack() as esx2:
            sXn2 = esx2.enter_context(nc.sbuf_tensor("sXn2", [P, NCH_D * TPC], BF16))
            emit_ln("2", sXn2, fence)
            v_ln2 = c.cnt("v_ln")

            @block.gpsimd
            def _(g):
                c.wge(g, "v_ln", v_ln2)
                for ch in range(NCH_D):
                    c.dma(g, ag_in[ch * P:(ch + 1) * P, :], sXn2[:, ch * TPC:(ch + 1) * TPC], "g_ag")
                c.wge(g, "g_ag", c.cnt("g_ag"))
                cc_i = g.collective_compute(
                    "AllGather", ALU.bypass,
                    replica_groups=[list(range(NCORES))],
                    ins=[ag_in.ap().opt()], outs=[ag_out.ap().opt()],
                )
                cc_i.then_inc(c.sem("cag"))
                c.n["cag"] += 1
                c.inc(g.engine_nop(), "g_ms")

        if _STOP_PHASE < 6:
            _emit_taps()
            return nc
        with ExitStack() as esm:
            if True:
                sRhs = esm.enter_context(nc.sbuf_tensor("sRhs", [P, NCH_D * TPC], BF16))
                sU4 = esm.enter_context(nc.sbuf_tensor("sU4", [P, NDT * TPC], BF16))
                sHt = esm.enter_context(nc.sbuf_tensor("sHt", [P, TPC + 4], F32))
                sCac = esm.enter_context(nc.sbuf_tensor("sCac", [P, TPC], F32))
                sTails = esm.enter_context(nc.sbuf_tensor("sTails", [P, NDT * 4], F32))
                sGe = esm.enter_context(nc.sbuf_tensor("sGe", [P, TPC], BF16))
                sXe = esm.enter_context(nc.sbuf_tensor("sXe", [P, TPC], F32))
                sXe2 = esm.enter_context(nc.sbuf_tensor("sXe2", [32, TPC], F32))
                psP = [esm.enter_context(nc.psum_tensor(f"psP{i}", [P, TPC], F32)) for i in range(2)]
                psX = esm.enter_context(nc.psum_tensor("psX", [P, TPC], F32))
                psX2 = esm.enter_context(nc.psum_tensor("psX2", [32, TPC], F32))
                b_lr = c.cnt("l_rhs")
                b_lw = c.cnt("l_wf")
                b_gc = c.cnt("g_wc")
                b_tp = c.cnt("t_pj")
                b_tx = c.cnt("t_wx")
                b_vc = c.cnt("v_cv")
                b_su = c.cnt("s_u")
                b_sg = c.cnt("s_g")
                b_vx = c.cnt("v_xe")
                b_gu = c.cnt("g_u")
                b_gg = c.cnt("g_g")
                b_gx = c.cnt("g_x")
                vfb_done = c.cnt("v_fb")
                if _DEBUG_TAPS:
                    c.dbg_sb += [("sXe", sXe, [P, TPC], F32), ("sXe2", sXe2, [32, TPC], F32),
                                 ("sU4", sU4, [P, NDT * TPC], BF16), ("sWxb", sWxb, [P, NDT * 160], BF16),
                                 ("sTails", sTails, [P, NDT * 4], F32), ("sHt", sHt, [P, TPC + 4], F32)]

                @block.vector
                def _(v):
                    c.inc(v.memset(sTails[:], 0.0), "v_tl")

                fence_fr = c.snap()

                @block.sync
                def _(s):
                    c.fence_wait(s, fence_fr)
                    s.wait_ge(c.sem("cag"), 1)
                    for cb in range(NCORES):
                        if cb >= 1:
                            c.wge(s, "t_pj", b_tp + 8 * (cb - 1) + 8)
                        for kc in range(NCH_D):
                            c.dma(s, sRhs[:, kc * TPC:(kc + 1) * TPC],
                                  ag_out[cb * D + kc * P: cb * D + (kc + 1) * P, :], "l_rhs")
                        for oc in range(8):
                            u = cb * 8 + oc
                            c.wge(s, "g_wc", b_gc + u - 1)
                            c.dma(s, sWf[u % 2][:], dr["wins"][oc, :, :], "l_wf")

                def emit_front_dmas(g, cb):
                    for dt in range(NDT):
                        idx = cb * 4 + dt
                        c.wge(g, "s_u", b_su + idx + 1)
                        c.dma(g, usp[dt * P:(dt + 1) * P, cb * TPC:(cb + 1) * TPC],
                              sU4[:, dt * TPC:(dt + 1) * TPC], "g_u")
                    for oc in range(4, 8):
                        gidx = cb * 4 + (oc - 4)
                        dtg = oc - 4
                        c.wge(g, "s_g", b_sg + gidx + 1)
                        c.dma(g, gsp[dtg * P:(dtg + 1) * P, cb * TPC:(cb + 1) * TPC], sGe[:], "g_g")
                    c.wge(g, "v_xe", b_vx + 2 * (cb + 1))
                    c.dma(g, ar_in[0:P, cb * TPC:(cb + 1) * TPC], sXe[:], "g_x")
                    c.dma(g, ar_in[P:160, cb * TPC:(cb + 1) * TPC], sXe2[:], "g_x")

                @block.gpsimd
                def _(g):
                    for cb in range(NCORES):
                        if cb >= 1:
                            emit_front_dmas(g, cb - 1)
                        for oc in range(8):
                            u = cb * 8 + oc
                            c.wge(g, "l_wf", b_lw + 16 * (u + 1))
                            c.wge(g, "t_pj", b_tp + u - 1)
                            c.inc(g.tensor_copy(out=sWb[u % 2][:], in_=sWf[u % 2][:]), "g_wc")
                    emit_front_dmas(g, NCORES - 1)
                    c.wge(g, "g_x", b_gx + 16 * 16)
                    cc_i = g.collective_compute(
                        "AllReduce", ALU.add,
                        replica_groups=[list(range(NCORES))],
                        ins=[ar_in.ap().opt()], outs=[ar_out.ap().opt()],
                    )
                    cc_i.then_inc(c.sem("car"))
                    c.n["car"] += 1
                    c.inc(g.engine_nop(), "g_ms")

                @block.tensor
                def _(t):
                    c.wge(t, "v_fb", vfb_done)
                    c.wge(t, "v_tl", 1)
                    for cb in range(NCORES):
                        c.wge(t, "l_rhs", b_lr + 16 * NCH_D * (cb + 1))
                        for oc in range(8):
                            u = cb * 8 + oc
                            c.wge(t, "g_wc", b_gc + u + 1)
                            up = u - 2
                            if up >= 0:
                                ocp, cbp = up % 8, up // 8
                                if ocp < 4:
                                    c.wge(t, "v_cv", b_vc + cbp * 4 + ocp + 1)
                                else:
                                    c.wge(t, "s_g", b_sg + cbp * 4 + (ocp - 4) + 1)
                            for kc in range(NCH_D):
                                for q in range(2):
                                    mm = t.matmul(psP[u % 2][:, q * 512:(q + 1) * 512],
                                                  sWb[u % 2][:, kc * P:(kc + 1) * P],
                                                  sRhs[:, kc * TPC + q * 512: kc * TPC + (q + 1) * 512],
                                                  start=(kc == 0), stop=(kc == NCH_D - 1))
                            c.inc(mm, "t_pj")
                        c.wge(t, "s_u", b_su + cb * 4 + 4)
                        c.wge(t, "v_xe", b_vx + 2 * cb)
                        for j in range(2):
                            w = P if j == 0 else 32
                            ps = psX if j == 0 else psX2
                            for dt in range(NDT):
                                for q in range(2):
                                    mm = t.matmul(ps[:, q * 512:(q + 1) * 512],
                                                  sWxb[:, dt * 160 + j * P: dt * 160 + j * P + w],
                                                  sU4[:, dt * TPC + q * 512: dt * TPC + (q + 1) * 512],
                                                  start=(dt == 0), stop=(dt == NDT - 1))
                        c.inc(mm, "t_wx")

                @block.vector
                def _(v):
                    for cb in range(NCORES):
                        for dt in range(NDT):
                            u = cb * 8 + dt
                            idx = cb * 4 + dt
                            c.wge(v, "t_pj", b_tp + u + 1)
                            c.wge(v, "s_u", b_su + idx)
                            v.tensor_copy(out=sHt[:, 0:3], in_=sTails[:, dt * 4:dt * 4 + 3])
                            v.tensor_copy(out=sTails[:, dt * 4:dt * 4 + 3],
                                          in_=psP[u % 2][:, TPC - 3:TPC])
                            v.tensor_copy(out=sHt[:, 3:3 + TPC], in_=psP[u % 2][:, :])
                            v.tensor_scalar(out=sCac[:], in0=sHt[:, 0:TPC],
                                            scalar1=sCw[:, dt * 4:dt * 4 + 1], scalar2=None, op0=ALU.mult)
                            for k in range(1, 4):
                                last = v.scalar_tensor_tensor(out=sCac[:], in0=sHt[:, k:k + TPC],
                                                              scalar=sCw[:, dt * 4 + k:dt * 4 + k + 1],
                                                              in1=sCac[:], op0=ALU.mult, op1=ALU.add)
                            c.inc(last, "v_cv")
                        c.wge(v, "t_wx", b_tx + cb + 1)
                        c.wge(v, "g_x", b_gx + 16 * 2 * cb)
                        v.tensor_copy(out=sXe[:], in_=psX[:, :])
                        c.inc(v.tensor_copy(out=sXe2[:], in_=psX2[:, :]), "v_xe", 2)
                    c.inc(v.engine_nop(), "v_ms")

                @block.scalar
                def _(sc):
                    for cb in range(NCORES):
                        for dt in range(NDT):
                            idx = cb * 4 + dt
                            c.wge(sc, "v_cv", b_vc + idx + 1)
                            c.wge(sc, "t_wx", b_tx + cb)
                            if cb >= 1:
                                c.wge(sc, "g_u", b_gu + 16 * (4 * (cb - 1) + dt + 1))
                            c.inc(sc.activation(out=sU4[:, dt * TPC:(dt + 1) * TPC], in_=sCac[:],
                                                func=AF.Silu, bias=sCb[:, dt:dt + 1]), "s_u")
                        for oc in range(4, 8):
                            u = cb * 8 + oc
                            gidx = cb * 4 + (oc - 4)
                            c.wge(sc, "t_pj", b_tp + u + 1)
                            c.wge(sc, "g_g", b_gg + 16 * gidx)
                            c.inc(sc.activation(out=sGe[:], in_=psP[u % 2][:, :], func=AF.Copy), "s_g")


                c.expect("t_pj", b_tp + 64)
                c.expect("v_cv", b_vc + 32)
                c.expect("s_u", b_su + 32)
                c.expect("s_g", b_sg + 32)
                c.expect("v_xe", b_vx + 16)
                c.expect("g_u", b_gu + 16 * 32)
                c.expect("g_g", b_gg + 16 * 32)
                c.expect("g_x", b_gx + 16 * 16)

        if _STOP_PHASE < 7:
            _emit_taps()
            return nc
        sTs = E(nc.sbuf_tensor("sTs", [P, L], BF16))
        with ExitStack() as esmid:
            sStg = esmid.enter_context(nc.sbuf_tensor("sStg", [P, 4096], F32))
            b_lt = c.cnt("l_ts")
            b_vt = c.cnt("v_ts")

            fence_mid = c.snap()

            @block.sync
            def _(s):
                c.fence_wait(s, fence_mid)
                s.wait_ge(c.sem("car"), 1)
                for i in range(2):
                    c.wge(s, "v_ts", b_vt + i)
                    c.dma(s, sStg[:], ar_out[0:P, i * 4096:(i + 1) * 4096], "l_ts")

            @block.vector
            def _(v):
                for i in range(2):
                    c.wge(v, "l_ts", b_lt + 16 * (i + 1))
                    c.inc(v.tensor_copy(out=sTs[:, i * 4096:(i + 1) * 4096], in_=sStg[:]), "v_ts")
                c.inc(v.engine_nop(), "v_ms")
            c.expect("v_ts", b_vt + 2)

        if _STOP_PHASE < 8:
            _emit_taps()
            return nc
        fence = c.snap()
        with ExitStack() as ess:
            sBCs = ess.enter_context(nc.sbuf_tensor("sBCs", [32, TSC], F32))
            sBCt = ess.enter_context(nc.sbuf_tensor("sBCt", [32, TSC], BF16))
            sBbc = ess.enter_context(nc.sbuf_tensor("sBbc", [P, NS * TSC], BF16))
            sCbc = ess.enter_context(nc.sbuf_tensor("sCbc", [P, NS * TSC], BF16))
            sDtSl = ess.enter_context(nc.sbuf_tensor("sDtSl", [P, TSC], F32))
            sT1 = ess.enter_context(nc.sbuf_tensor("sT1", [P, TSC], F32))
            sT2 = ess.enter_context(nc.sbuf_tensor("sT2", [P, TSC], F32))
            sDtu = ess.enter_context(nc.sbuf_tensor("sDtu", [P, TSC], BF16))
            sdBu = ess.enter_context(nc.sbuf_tensor("sdBu", [P, TSC], BF16))
            sdA = [ess.enter_context(nc.sbuf_tensor(f"sdA{i}", [P, TSC], F32)) for i in range(2)]
            sSoA = ess.enter_context(nc.sbuf_tensor("sSoA", [P, NS * TSC], BF16))
            sSC = ess.enter_context(nc.sbuf_tensor("sSC", [P, TSC], BF16))
            sUt = [ess.enter_context(nc.sbuf_tensor(f"sUt{i}", [P, TSC], BF16)) for i in range(2)]
            sGt = [ess.enter_context(nc.sbuf_tensor(f"sGt{i}", [P, TSC], BF16)) for i in range(2)]
            sSg = ess.enter_context(nc.sbuf_tensor("sSg", [P, TSC], F32))
            sY1 = ess.enter_context(nc.sbuf_tensor("sY1", [P, TSC], F32))
            sYp = ess.enter_context(nc.sbuf_tensor("sYp", [P, TSC], BF16))
            psBc = [ess.enter_context(nc.psum_tensor(f"psBc{i}", [P, TSC], F32)) for i in range(2)]
            psDt = ess.enter_context(nc.psum_tensor("psDt", [P, TSC], F32))
            psY = [ess.enter_context(nc.psum_tensor(f"psYs{i}", [P, TSC], F32)) for i in range(2)]
            b_lu = c.cnt("l_ug")
            b_lbc = c.cnt("l_bc")
            b_vcc = c.cnt("v_bcc")
            b_tb = c.cnt("t_bc")
            b_vb = c.cnt("v_bc")
            b_td = c.cnt("t_dt")
            b_sp = c.cnt("s_sp")
            b_vd = c.cnt("v_dts")
            b_sa = c.cnt("s_dA")
            b_vn = c.cnt("v_scn")
            b_vq = c.cnt("v_sc")
            b_tr = c.cnt("t_rd")
            b_vs = c.cnt("v_stt")
            b_vy = c.cnt("v_yp")
            b_ssg = c.cnt("s_sg")
            b_gy = c.cnt("g_yp")

            @block.sync
            def _(s):
                c.fence_wait(s, fence)
                for k in range(NTCH * NDT):
                    tch, dt = k // NDT, k % NDT
                    if dt == 0:
                        c.wge(s, "v_bcc", b_vcc + tch)
                        c.dma(s, sBCs[:], ar_out[P:160, tch * TSC:(tch + 1) * TSC], "l_bc")
                    c.wge(s, "v_stt", b_vs + k - 1)
                    c.wge(s, "s_sg", b_ssg + k - 1)
                    c.dma(s, sUt[k % 2][:], usp[dt * P:(dt + 1) * P, tch * TSC:(tch + 1) * TSC], "l_ug")
                    c.dma(s, sGt[k % 2][:], gsp[dt * P:(dt + 1) * P, tch * TSC:(tch + 1) * TSC], "l_ug")

            @block.tensor
            def _(t):
                c.fence_wait(t, fence)
                for tch in range(NTCH):
                    c.wge(t, "v_bcc", b_vcc + tch + 1)
                    for n in range(2 * NS):
                        bn = tch * 2 * NS + n
                        c.wge(t, "v_bc", b_vb + bn - 1)
                        mm = t.matmul(psBc[bn % 2][:, :], sEsel[:, n * P:(n + 1) * P],
                                      sBCt[:, :], start=True, stop=True)
                        c.inc(mm, "t_bc")
                    for dt in range(NDT):
                        k = tch * NDT + dt
                        c.wge(t, "s_sp", b_sp + k)
                        mm = t.matmul(psDt[:, :], sWdtb[:, dt * P:(dt + 1) * P],
                                      sTs[:, tch * TSC:(tch + 1) * TSC], start=True, stop=True)
                        c.inc(mm, "t_dt")
                        c.wge(t, "v_stt", b_vs + k - 1)
                        for n in range(NS):
                            c.wge(t, "v_sc", b_vq + k * NS + n + 1)
                            mm = t.matmul(psY[k % 2][:, :], sEyeB[:], sSC[:, :],
                                          start=(n == 0), stop=(n == NS - 1))
                            c.inc(mm, "t_rd")

            @block.scalar
            def _(sc):
                c.fence_wait(sc, fence)
                for tch in range(NTCH):
                    for dt in range(NDT):
                        k = tch * NDT + dt
                        c.wge(sc, "t_dt", b_td + k + 1)
                        c.wge(sc, "v_dts", b_vd + k)
                        sc.activation(out=sT1[:], in_=psDt[:, :], func=AF.Abs, bias=sBdt[:, dt:dt + 1])
                        sc.activation(out=sT2[:], in_=sT1[:], func=AF.Exp, scale=-1.0)
                        sc.activation(out=sT2[:], in_=sT2[:], func=AF.Ln, bias=1.0)
                        c.inc(sc.activation(out=sT1[:], in_=psDt[:, :], func=AF.Relu,
                                            bias=sBdt[:, dt:dt + 1]), "s_sp")
                        c.wge(sc, "v_dts", b_vd + k + 1)
                        for n in range(NS):
                            if n >= 2:
                                c.wge(sc, "v_scn", b_vn + k * NS + n - 1)
                            c.inc(sc.activation(out=sdA[n % 2][:], in_=sDtSl[:], func=AF.Exp,
                                                scale=sA[:, dt * NS + n: dt * NS + n + 1]), "s_dA")
                        c.wge(sc, "l_ug", b_lu + 16 * (2 * k + 2))
                        c.wge(sc, "v_yp", b_vy + k)
                        c.inc(sc.activation(out=sSg[:], in_=sGt[k % 2][:], func=AF.Silu), "s_sg")

            @block.vector
            def _(v):
                c.fence_wait(v, fence)
                for tch in range(NTCH):
                    c.wge(v, "l_bc", b_lbc + 16 * (tch + 1))
                    c.wge(v, "t_bc", b_tb + 32 * tch)
                    c.inc(v.tensor_copy(out=sBCt[:], in_=sBCs[:]), "v_bcc")
                    for n in range(2 * NS):
                        bn = tch * 2 * NS + n
                        c.wge(v, "t_bc", b_tb + bn + 1)
                        dst = sBbc if n < NS else sCbc
                        nn = n % NS
                        c.inc(v.tensor_copy(out=dst[:, nn * TSC:(nn + 1) * TSC],
                                            in_=psBc[bn % 2][:, :]), "v_bc")
                    for dt in range(NDT):
                        k = tch * NDT + dt
                        c.wge(v, "s_sp", b_sp + k + 1)
                        c.wge(v, "l_ug", b_lu + 16 * (2 * k + 1))
                        v.tensor_add(out=sDtSl[:], in0=sT1[:], in1=sT2[:])
                        c.inc(v.tensor_tensor(out=sDtu[:], in0=sDtSl[:], in1=sUt[k % 2][:],
                                              op=ALU.mult), "v_dts")
                        for n in range(NS):
                            v.tensor_tensor(out=sdBu[:], in0=sDtu[:],
                                            in1=sBbc[:, n * TSC:(n + 1) * TSC], op=ALU.mult)
                            c.wge(v, "s_dA", b_sa + k * NS + n + 1)
                            c.inc(v.tensor_tensor_scan(
                                out=sSoA[:, n * TSC:(n + 1) * TSC], data0=sdA[n % 2][:],
                                data1=sdBu[:], initial=sState[:, dt * NS + n: dt * NS + n + 1],
                                op0=ALU.mult, op1=ALU.add), "v_scn")
                            c.wge(v, "t_rd", b_tr + k * NS + n)
                            c.inc(v.tensor_tensor(out=sSC[:], in0=sSoA[:, n * TSC:(n + 1) * TSC],
                                                  in1=sCbc[:, n * TSC:(n + 1) * TSC], op=ALU.mult),
                                  "v_sc")
                        c.wge(v, "t_rd", b_tr + (k + 1) * NS)
                        c.inc(v.scalar_tensor_tensor(out=sY1[:], in0=sUt[k % 2][:],
                                                     scalar=sDp[:, dt:dt + 1], in1=psY[k % 2][:, :],
                                                     op0=ALU.mult, op1=ALU.add), "v_stt")
                        src = sSoA[:].rearrange("p (n t) -> p n t", n=NS)[:, :, TSC - 1:TSC]
                        v.tensor_copy(out=sState[:, dt * NS:(dt + 1) * NS], in_=src)
                        c.wge(v, "s_sg", b_ssg + k + 1)
                        c.wge(v, "g_yp", b_gy + 16 * k)
                        c.inc(v.tensor_tensor(out=sYp[:], in0=sY1[:], in1=sSg[:], op=ALU.mult), "v_yp")
                c.inc(v.engine_nop(), "v_ms")

            @block.gpsimd
            def _(g):
                for k in range(NTCH * NDT):
                    tch, dt = k // NDT, k % NDT
                    c.wge(g, "v_yp", b_vy + k + 1)
                    dst_core = tch // 2
                    coff = (tch % 2) * TSC
                    c.dma(g, a2a_in[dst_core * DSC + dt * P: dst_core * DSC + (dt + 1) * P,
                                    coff:coff + TSC], sYp[:], "g_yp")
                c.wge(g, "g_yp", b_gy + 16 * NTCH * NDT)
                with allow_ici_all_to_all():
                    cc_i = g.collective_compute(
                        "AllToAll", ALU.bypass,
                        replica_groups=[list(range(NCORES))],
                        ins=[a2a_in.ap().opt()], outs=[a2a_out.ap().opt()],
                    )
                cc_i.then_inc(c.sem("ca2a"))
                c.n["ca2a"] += 1
                c.inc(g.engine_nop(), "g_ms")
            c.expect("t_bc", b_tb + NTCH * 2 * NS)
            c.expect("v_bc", b_vb + NTCH * 2 * NS)
            c.expect("t_rd", b_tr + NTCH * NDT * NS)
            c.expect("v_scn", b_vn + NTCH * NDT * NS)
            c.expect("v_sc", b_vq + NTCH * NDT * NS)
            c.expect("v_stt", b_vs + NTCH * NDT)
            c.expect("v_yp", b_vy + NTCH * NDT)
            c.expect("s_dA", b_sa + NTCH * NDT * NS)
            c.expect("s_sg", b_ssg + NTCH * NDT)
            c.expect("g_yp", b_gy + 16 * NTCH * NDT)

        if _STOP_PHASE < 9:
            _emit_taps()
            return nc
        fence = c.snap()
        with ExitStack() as eso:
            sYr = [eso.enter_context(nc.sbuf_tensor(f"sYr{i}", [P, 512], BF16)) for i in range(2)]
            psW = [eso.enter_context(nc.psum_tensor(f"psWo{j}", [P, 512], F32)) for j in range(4)]
            b_lw = c.cnt("l_wf")
            b_lr = c.cnt("l_yr")
            b_gc = c.cnt("g_wc")
            b_tw = c.cnt("t_wo")
            b_vw = c.cnt("v_wo")
            NU = 256

            @block.sync
            def _(s):
                c.fence_wait(s, fence)
                s.wait_ge(c.sem("ca2a"), 1)
                for u in range(NU):
                    tc2, dg, kc = u // 128, (u // 32) % 4, u % 32
                    c.wge(s, "g_wc", b_gc + u - 1)
                    c.dma(s, sWf[u % 2][:, 0:512], dr["wouts"][dg, kc, :, :], "l_wf")
                    c.wge(s, "t_wo", b_tw + u - 1)
                    c.dma(s, sYr[u % 2][:], a2a_out[kc * P:(kc + 1) * P, tc2 * 512:(tc2 + 1) * 512],
                          "l_yr")

            @block.gpsimd
            def _(g):
                for u in range(NU):
                    c.wge(g, "l_wf", b_lw + 16 * (u + 1))
                    c.wge(g, "t_wo", b_tw + u - 1)
                    c.inc(g.tensor_copy(out=sWb[u % 2][:, 0:512], in_=sWf[u % 2][:, 0:512]), "g_wc")
                c.inc(g.engine_nop(), "g_ms")

            @block.tensor
            def _(t):
                c.fence_wait(t, fence)
                for u in range(NU):
                    tc2, dg, kc = u // 128, (u // 32) % 4, u % 32
                    grp = u // 32
                    c.wge(t, "g_wc", b_gc + u + 1)
                    c.wge(t, "l_yr", b_lr + 16 * (u + 1))
                    if kc == 0 and grp >= 1:
                        c.wge(t, "v_wo", b_vw + grp)
                    for j in range(4):
                        mm = t.matmul(psW[j][:, :], sWb[u % 2][:, j * P:(j + 1) * P], sYr[u % 2][:],
                                      start=(kc == 0), stop=(kc == 31))
                    c.inc(mm, "t_wo")

            @block.vector
            def _(v):
                c.fence_wait(v, fence)
                for grp in range(8):
                    tc2, dg = grp // 4, grp % 4
                    c.wge(v, "t_wo", b_tw + (grp + 1) * 32)
                    for j in range(4):
                        dc = dg * 4 + j
                        sl = sR[:, dc * TPC + tc2 * 512: dc * TPC + (tc2 + 1) * 512]
                        last = v.scalar_tensor_tensor(out=sl, in0=psW[j][:, :], scalar=1.0, in1=sl,
                                                      op0=ALU.mult, op1=ALU.add)
                    c.inc(last, "v_wo")
                c.inc(v.engine_nop(), "v_ms")
            c.expect("t_wo", b_tw + NU)
            c.expect("v_wo", b_vw + 8)

        if _STOP_PHASE < 10:
            _emit_taps()
            return nc
        fence = c.snap()
        with ExitStack() as esx:
            sXn = esx.enter_context(nc.sbuf_tensor("sXn3", [P, NCH_D * TPC], BF16))
            emit_ln("3", sXn, fence)
            fence = c.snap()
            ffn2_bgs = emit_ffn_a(False, sXn, fence)
        emit_ffn_b(False, ffn2_bgs)

        if _STOP_PHASE < 11:
            _emit_taps()
            return nc
        fence = c.snap()
        with ExitStack() as eso:
            sOut = eso.enter_context(nc.sbuf_tensor("sOut", [P, D], F32))
            psT = eso.enter_context(nc.psum_tensor("psT1", [P, D], F32))
            b_to = c.cnt("t_ot")
            b_vo = c.cnt("v_ot")
            b_go = c.cnt("g_ot")

            @block.tensor
            def _(t):
                c.fence_wait(t, fence)
                for tt in range(8):
                    c.wge(t, "v_ot", b_vo + tt)
                    for ch in range(NCH_D):
                        mm = t.transpose(psT[:, ch * P:(ch + 1) * P],
                                         sR[:, ch * TPC + tt * P: ch * TPC + (tt + 1) * P], sEyeF[:])
                    c.inc(mm, "t_ot")

            @block.vector
            def _(v):
                c.fence_wait(v, fence)
                for tt in range(8):
                    c.wge(v, "t_ot", b_to + tt + 1)
                    c.wge(v, "g_ot", b_go + 16 * tt)
                    c.inc(v.tensor_copy(out=sOut[:], in_=psT[:, :]), "v_ot")

            @block.gpsimd
            def _(g):
                for tt in range(8):
                    c.wge(g, "v_ot", b_vo + tt + 1)
                    c.dma(g, out_t[tt * P:(tt + 1) * P, :], sOut[:], "g_ot")
                c.wge(g, "g_ot", b_go + 16 * 8)
            c.expect("t_ot", b_to + 8)
            c.expect("v_ot", b_vo + 8)
            c.expect("g_ot", b_go + 16 * 8)

        _emit_taps()

    return nc


def _host_inputs(inputs):
    x = np.asarray(inputs["L0_attn_hidden_states"])[0]
    w_in = np.asarray(inputs["L0_attn_W_in"])
    conv_w = np.asarray(inputs["L0_attn_conv_w"])[:, 0, :]
    conv_b = np.asarray(inputs["L0_attn_conv_b"])
    w_x = np.asarray(inputs["L0_attn_W_x"])
    w_dt = np.asarray(inputs["L0_attn_W_dt"])
    b_dt = np.asarray(inputs["L0_attn_b_dt"])
    a_log = np.asarray(inputs["L0_attn_A_log"])
    d_par = np.asarray(inputs["L0_attn_D_param"])
    w_out = np.asarray(inputs["L0_attn_W_out"])
    w1 = np.asarray(inputs["L0_ffn_W1"])
    w2 = np.asarray(inputs["L0_ffn_W2"])
    w3 = np.asarray(inputs["L0_ffn_W3"])

    def stream_oi(wT):
        K, O = wT.shape
        nkc, noc = K // P, O // P
        return np.ascontiguousarray(
            wT.reshape(nkc, P, noc, P).transpose(2, 1, 0, 3).reshape(noc, P, nkc * P))

    w1s = stream_oi(np.ascontiguousarray(w1.T))
    w3s = stream_oi(np.ascontiguousarray(w3.T))
    w2T = np.ascontiguousarray(w2.T)
    w2g = np.ascontiguousarray(w2T.reshape(64, P, D).transpose(1, 0, 2).reshape(HF, D))
    woutT = np.ascontiguousarray(w_out.T)
    wouts = np.ascontiguousarray(woutT.reshape(32, P, 4, 512).transpose(2, 0, 1, 3))

    h64 = _had(64)
    h128 = _had(128).astype(BF)
    bd64 = np.zeros((P, P), np.float32)
    bd64[0:64, 0:64] = h64
    bd64[64:128, 64:128] = h64
    bd64 = bd64.astype(BF)
    eyef = np.eye(P, dtype=np.float32)
    eyeb = eyef.astype(BF)
    onesr = np.ones((1, P), np.float32).astype(BF)
    onesc = np.ones((P, 1), np.float32)
    esel = np.zeros((32, 32 * P), np.float32)
    for n in range(NS):
        esel[n, n * P:(n + 1) * P] = 1.0
        esel[NS + n, (NS + n) * P:(NS + n + 1) * P] = 1.0
    esel = esel.astype(BF)

    winT = np.ascontiguousarray(w_in.T)
    in_maps = []
    for cc in range(NCORES):
        ds = slice(cc * DSC, (cc + 1) * DSC)
        win_slice = np.concatenate(
            [winT[:, cc * DSC:(cc + 1) * DSC], winT[:, DM + cc * DSC: DM + (cc + 1) * DSC]], axis=1)
        wins = np.ascontiguousarray(
            win_slice.reshape(NCH_D, P, 8, P).transpose(2, 1, 0, 3).reshape(8, P, NCH_D * P))
        in_maps.append(dict(
            x_t=np.ascontiguousarray(x[cc * TPC:(cc + 1) * TPC, :]),
            w1s=w1s, w3s=w3s, w2g=w2g, wouts=wouts, wins=wins,
            wxs=np.ascontiguousarray(w_x.T[ds, :].reshape(NDT, P, 160).transpose(1, 0, 2).reshape(P, -1)),
            wdts=np.ascontiguousarray(w_dt.T[:, ds]),
            cws=np.ascontiguousarray(conv_w[ds, :].reshape(NDT, P, 4).transpose(1, 0, 2).reshape(P, -1)),
            cbs=np.ascontiguousarray(conv_b[ds].reshape(NDT, P).T),
            bdts=np.ascontiguousarray(b_dt[ds].reshape(NDT, P).T),
            alogs=np.ascontiguousarray(a_log[ds, :].reshape(NDT, P, NS).transpose(1, 0, 2).reshape(P, -1)),
            dpars=np.ascontiguousarray(d_par[ds].reshape(NDT, P).T),
            h128=h128, bd64=bd64, eyef=eyef, eyeb=eyeb, onesr=onesr, onesc=onesc,
            esel=esel,
        ))
    return in_maps


def kernel(**inputs):
    in_maps = _host_inputs(inputs)
    nc = build_program()
    res = run_bass_kernel_spmd(nc, in_maps, core_ids=list(range(NCORES)))
    out = np.concatenate([res.results[cc]["out_t"] for cc in range(NCORES)], axis=0)
    return out.reshape(1, L, D).astype(np.float32)


if __name__ == "__main__":
    nc = build_program()
    print("built ok;", nc.n_instructions(), "instructions")


# version 08
# speedup vs baseline: 1.0338x; 1.0338x over previous
import numpy as np
import ml_dtypes
import concourse.bass as bass
import concourse.mybir as mybir
from concourse.bass_utils import run_bass_kernel_spmd
from concourse.aot_env import allow_ici_all_to_all
from contextlib import ExitStack

F32 = mybir.dt.float32
BF16 = mybir.dt.bfloat16
AF = mybir.ActivationFunctionType
ALU = mybir.AluOpType
BF = ml_dtypes.bfloat16

NCORES = 8
P = 128
L = 8192
D = 2048
HF = 8192
DM = 4096
NS = 16
RDT = 128
TPC = L // NCORES
DSC = DM // NCORES
NCH_D = D // P
NCH_H = HF // P
NDT = DSC // P
TSC = 512
NTCH = L // TSC
INV_SQRT_N = float(1.0 / np.sqrt(HF))

FENCE_SEMS = ["t_ms", "v_ms", "s_ms", "g_ms"]
_STOP_PHASE = 99
_DEBUG_TAPS = False


def _had(n):
    h = np.array([[1.0]], dtype=np.float32)
    while h.shape[0] < n:
        h = np.block([[h, h], [h, -h]])
    return h


class Ctx:
    def __init__(self, nc, es):
        self.nc = nc
        self.es = es
        self.h = {}
        self.n = {}

    def sem(self, name):
        if name not in self.h:
            self.h[name] = self.es.enter_context(self.nc.semaphore(name))
            self.n[name] = 0
        return self.h[name]

    def cnt(self, name):
        self.sem(name)
        return self.n[name]

    def inc(self, instr, name, amt=1):
        instr.then_inc(self.sem(name), amt)
        self.n[name] += amt
        return self.n[name]

    def dma(self, eng, out, in_, name):
        return self.inc(eng.dma_start(out=out, in_=in_), name, 16)

    def wge(self, eng, name, val):
        if val > 0:
            eng.wait_ge(self.sem(name), val)

    def snap(self):
        return dict(self.n)

    def fence_wait(self, eng, snap):
        for k, v in snap.items():
            if v > 0:
                eng.wait_ge(self.sem(k), v)

    def expect(self, name, val):
        assert self.n[name] == val, f"sem {name}: built {self.n[name]} != predicted {val}"


def build_program():
    nc = bass.Bass("TRN2", target_bir_lowering=False)
    dr = {}

    def din(name, shape, dt=F32):
        dr[name] = nc.dram_tensor(name, list(shape), dt, kind="ExternalInput")
        return dr[name]

    din("x_t", (TPC, D))
    din("w1s", (NCH_H, P, NCH_D * P))
    din("w3s", (NCH_H, P, NCH_D * P))
    din("w2g", (HF, D))
    din("wouts", (4, 32, P, 512))
    din("wins", (8, P, NCH_D * P))
    din("wxs", (P, NDT * 160))
    din("wdts", (RDT, DSC))
    din("cws", (P, NDT * 4))
    din("cbs", (P, NDT))
    din("bdts", (P, NDT))
    din("alogs", (P, NDT * NS))
    din("dpars", (P, NDT))
    din("h128", (P, P), BF16)
    din("bd64", (P, P), BF16)
    din("eyef", (P, P))
    din("eyeb", (P, P), BF16)
    din("onesr", (1, P), BF16)
    din("onesc", (P, 1))
    din("esel", (32, 32 * P), BF16)

    out_t = nc.dram_tensor("out_t", [TPC, D], F32, kind="ExternalOutput")
    if _DEBUG_TAPS:
        dbg = {nm: nc.dram_tensor("dbg_" + nm, shp, dt, kind="ExternalOutput")
               for nm, shp, dt in [
                   ("ag", [NCORES * D, TPC], BF16), ("ar", [160, L], F32),
                   ("a2a", [NCORES * DSC, TPC], BF16), ("usp", [DSC, L], BF16),
                   ("gsp", [DSC, L], BF16), ("w2p", [HF, D], BF16),
                   ("s1", [NCH_H, P, TPC], BF16), ("ari", [160, L], F32)]}

    w1bf = nc.dram_tensor("w1bf", [NCH_H, P, NCH_D * P], BF16)
    w3bf = nc.dram_tensor("w3bf", [NCH_H, P, NCH_D * P], BF16)
    w2pbf = nc.dram_tensor("w2pbf", [HF, D], BF16)
    s1sp = nc.dram_tensor("s1sp", [NCH_H, P, TPC], BF16)
    usp = nc.dram_tensor("usp", [DSC, L], BF16)
    gsp = nc.dram_tensor("gsp", [DSC, L], BF16)
    ag_in = nc.dram_tensor("ag_in", [D, TPC], BF16)
    ag_out = nc.dram_tensor("ag_out", [NCORES * D, TPC], BF16)
    ar_in = nc.dram_tensor("ar_in", [160, L], F32)
    ar_out = nc.dram_tensor("ar_out", [160, L], F32)
    a2a_in = nc.dram_tensor("a2a_in", [NCORES * DSC, TPC], BF16)
    a2a_out = nc.dram_tensor("a2a_out", [NCORES * DSC, TPC], BF16)

    es = ExitStack()
    with es:
        c = Ctx(nc, es)
        E = es.enter_context

        sR = E(nc.sbuf_tensor("sR", [P, NCH_D * TPC], F32))
        sWf = [E(nc.sbuf_tensor(f"sWf{i}", [P, 2048], F32)) for i in range(2)]
        sWb = [E(nc.sbuf_tensor(f"sWb{i}", [P, 2048], BF16)) for i in range(2)]
        sH128 = E(nc.sbuf_tensor("sH128", [P, P], BF16))
        sBD64 = E(nc.sbuf_tensor("sBD64", [P, P], BF16))
        sEyeF = E(nc.sbuf_tensor("sEyeF", [P, P], F32))
        sEyeB = E(nc.sbuf_tensor("sEyeB", [P, P], BF16))
        sOnesR = E(nc.sbuf_tensor("sOnesR", [1, P], BF16))
        sOnesC = E(nc.sbuf_tensor("sOnesC", [P, 1], F32))
        sEsel = E(nc.sbuf_tensor("sEsel", [32, 32 * P], BF16))
        sA = E(nc.sbuf_tensor("sA", [P, NDT * NS], F32))
        sAtmp = E(nc.sbuf_tensor("sAtmp", [P, NDT * NS], F32))
        sCw = E(nc.sbuf_tensor("sCw", [P, NDT * 4], F32))
        sCb = E(nc.sbuf_tensor("sCb", [P, NDT], F32))
        sBdt = E(nc.sbuf_tensor("sBdt", [P, NDT], F32))
        sDp = E(nc.sbuf_tensor("sDp", [P, NDT], F32))
        sState = E(nc.sbuf_tensor("sState", [P, NDT * NS], F32))
        sEps = E(nc.sbuf_tensor("sEps", [P, 1], F32))
        sWdtb = E(nc.sbuf_tensor("sWdtb", [P, DSC], BF16))
        sWxb = E(nc.sbuf_tensor("sWxb", [P, NDT * 160], BF16))
        es0w = ExitStack()
        sWdtS = es0w.enter_context(nc.sbuf_tensor("sWdtS", [P, DSC], F32))
        sWxS = es0w.enter_context(nc.sbuf_tensor("sWxS", [P, NDT * 160], F32))

        block = E(nc.Block())

        c.dbg_sb = []

        def _emit_taps():
            if not _DEBUG_TAPS:
                return
            sems = list(c.h.items())
            counts = {k: c.n[k] for k, _ in sems}

            @block.gpsimd
            def _(g):
                for k, v in counts.items():
                    if v > 0 and not k.startswith("c"):
                        g.wait_ge(c.h[k], v)
                for nm, scr in [("ag", ag_out), ("ar", ar_out), ("a2a", a2a_out),
                                ("usp", usp), ("gsp", gsp), ("w2p", w2pbf), ("s1", s1sp),
                                ("ari", ar_in)]:
                    c.dma(g, dbg[nm].ap(), scr.ap(), "g_dbg")
                for nm, hnd, shp, dt in c.dbg_sb:
                    od = nc.dram_tensor("dbgsb_" + nm, list(shp), dt, kind="ExternalOutput")
                    c.dma(g, od.ap(), hnd[:], "g_dbg")
                c.wge(g, "g_dbg", c.cnt("g_dbg"))

        @block.sync
        def _(s):
            c.dma(s, sH128[:], dr["h128"][:, :], "q_ld")
            c.dma(s, sBD64[:], dr["bd64"][:, :], "q_ld")
            c.dma(s, sEyeF[:], dr["eyef"][:, :], "q_ld")
            c.dma(s, sEyeB[:], dr["eyeb"][:, :], "q_ld")
            c.dma(s, sOnesR[:], dr["onesr"][:, :], "q_ld")
            c.dma(s, sOnesC[:], dr["onesc"][:, :], "q_ld")
            c.dma(s, sEsel[:], dr["esel"][:, :], "q_ld")
            c.dma(s, sAtmp[:], dr["alogs"][:, :], "q_ld")
            c.dma(s, sCw[:], dr["cws"][:, :], "q_ld")
            c.dma(s, sCb[:], dr["cbs"][:, :], "q_ld")
            c.dma(s, sBdt[:], dr["bdts"][:, :], "q_ld")
            c.dma(s, sDp[:], dr["dpars"][:, :], "q_ld")
            c.dma(s, sWdtS[:], dr["wdts"][:, :], "q_ld")
            c.dma(s, sWxS[:], dr["wxs"][:, :], "q_ld")
        ld_consts = c.cnt("q_ld")

        @block.scalar
        def _(sc):
            c.wge(sc, "q_ld", ld_consts)
            c.inc(sc.activation(out=sA[:], in_=sAtmp[:], func=AF.Exp), "s_ms")
        s_exp = c.cnt("s_ms")

        @block.vector
        def _(v):
            c.wge(v, "s_ms", s_exp)
            v.tensor_scalar(out=sA[:], in0=sA[:], scalar1=-1.0, scalar2=None, op0=ALU.mult)
            c.wge(v, "q_ld", ld_consts)
            v.tensor_copy(out=sWdtb[:], in_=sWdtS[:])
            v.tensor_copy(out=sWxb[:], in_=sWxS[:])
            v.memset(sEps[:], 1e-5)
            c.inc(v.memset(sState[:], 0.0), "v_ms")
        es0w.close()

        if _STOP_PHASE < 1:
            _emit_taps()
            return nc
        with ExitStack() as es0:
            sXin = [es0.enter_context(nc.sbuf_tensor(f"sXin{i}", [P, D], F32)) for i in range(2)]
            psT = es0.enter_context(nc.psum_tensor("psT0", [P, D], F32))

            fence0b = c.snap()

            @block.sync
            def _(s):
                c.fence_wait(s, fence0b)
                for tt in range(8):
                    c.wge(s, "t_x", tt - 1)
                    c.dma(s, sXin[tt % 2][:], dr["x_t"][tt * P:(tt + 1) * P, :], "l_x")

            @block.tensor
            def _(t):
                c.wge(t, "q_ld", ld_consts)
                for tt in range(8):
                    c.wge(t, "l_x", 16 * (tt + 1))
                    c.wge(t, "v_x", tt)
                    for ch in range(NCH_D):
                        mm = t.transpose(psT[:, ch * P:(ch + 1) * P],
                                         sXin[tt % 2][:, ch * P:(ch + 1) * P], sEyeF[:])
                    c.inc(mm, "t_x")

            @block.vector
            def _(v):
                for tt in range(8):
                    c.wge(v, "t_x", tt + 1)
                    dst = sR[:].rearrange("p (ch t) -> p ch t", ch=NCH_D)[:, :, tt * P:(tt + 1) * P]
                    src = psT[:].rearrange("p (ch t) -> p ch t", ch=NCH_D)
                    c.inc(v.tensor_copy(out=dst, in_=src), "v_x")
                c.inc(v.engine_nop(), "v_ms")
            c.expect("t_x", 8)
            c.expect("v_x", 8)

        if _STOP_PHASE < 2:
            _emit_taps()
            return nc
        with ExitStack() as esp:
            sW2e = esp.enter_context(nc.sbuf_tensor("sW2e", [P, D], BF16))
            psW2 = esp.enter_context(nc.psum_tensor("psW2", [P, D], F32))
            NPP = 64
            fence0c = c.snap()

            @block.sync
            def _(s):
                c.fence_wait(s, fence0c)
                for pp in range(NPP):
                    c.wge(s, "g_wc", pp - 1)
                    c.dma(s, sWf[pp % 2][:, 0:1024], dr["w2g"][pp * P:(pp + 1) * P, 0:1024], "l_wf")

            def emit_w2dma(g, pp):
                c.wge(g, "s_w2", pp + 1)
                dst = w2pbf.ap().rearrange("(cc p) d -> p cc d", p=P)[pp * 2:pp * 2 + 2, :, :]
                c.dma(g, dst, sW2e[:], "g_w2")

            @block.gpsimd
            def _(g):
                for pp in range(NPP):
                    c.wge(g, "g_wc", pp - 1)
                    c.dma(g, sWf[pp % 2][:, 1024:2048], dr["w2g"][pp * P:(pp + 1) * P, 1024:2048], "l_wp")
                    if pp >= 2:
                        emit_w2dma(g, pp - 2)
                    c.wge(g, "l_wf", 16 * (pp + 1))
                    c.wge(g, "l_wp", 16 * (pp + 1))
                    c.wge(g, "t_w2", pp - 1)
                    c.inc(g.tensor_copy(out=sWb[pp % 2][:], in_=sWf[pp % 2][:]), "g_wc")
                emit_w2dma(g, NPP - 2)
                emit_w2dma(g, NPP - 1)
                c.inc(g.engine_nop(), "g_ms")

            @block.tensor
            def _(t):
                c.fence_wait(t, fence0c)
                for pp in range(NPP):
                    c.wge(t, "g_wc", pp + 1)
                    c.wge(t, "s_w2", pp)
                    for q in range(4):
                        mm = t.matmul(psW2[:, q * 512:(q + 1) * 512], sBD64[:],
                                      sWb[pp % 2][:, q * 512:(q + 1) * 512], start=True, stop=True)
                    c.inc(mm, "t_w2")
                c.inc(mm, "t_ms") if False else None

            @block.scalar
            def _(sc):
                c.fence_wait(sc, fence0c)
                for pp in range(NPP):
                    c.wge(sc, "t_w2", pp + 1)
                    c.wge(sc, "g_w2", 16 * pp)
                    c.inc(sc.activation(out=sW2e[:], in_=psW2[:, :], func=AF.Copy), "s_w2")

            c.expect("t_w2", NPP)
            c.expect("s_w2", NPP)
            c.expect("g_w2", 16 * NPP)

        def emit_ln(tag, sXn, fence):
            with ExitStack() as esl:
                sSq = esl.enter_context(nc.sbuf_tensor(f"sSq{tag}", [P, TPC], F32))
                sStat = esl.enter_context(nc.sbuf_tensor(f"sStat{tag}", [1, 4 * TPC], F32))
                sStatB = esl.enter_context(nc.sbuf_tensor(f"sStatB{tag}", [1, 2 * TPC], BF16))
                psSum = esl.enter_context(nc.psum_tensor(f"psSum{tag}", [1, TPC], F32))
                psSq = esl.enter_context(nc.psum_tensor(f"psSq{tag}", [1, TPC], F32))
                psBrd = esl.enter_context(nc.psum_tensor(f"psBrd{tag}", [P, 2 * TPC], F32))
                b_tq = c.cnt("t_lnq")
                b_ts = c.cnt("t_lns")
                b_ss = c.cnt("s_lnq")
                b_vs = c.cnt("v_lns")
                b_sd = c.cnt("s_lnd")
                b_tb = c.cnt("t_lnb")
                b_vl = c.cnt("v_ln")

                MU = sStat[0:1, 0:TPC]
                MSQ = sStat[0:1, TPC:2 * TPC]
                MU2 = sStat[0:1, 2 * TPC:3 * TPC]
                VAR = sStat[0:1, TPC:2 * TPC]
                SD = sStat[0:1, TPC:2 * TPC]
                RSTD = sStat[0:1, TPC:2 * TPC]
                BNEG = sStat[0:1, 3 * TPC:4 * TPC]

                @block.scalar
                def _(sc):
                    c.fence_wait(sc, fence)
                    for ch in range(NCH_D):
                        c.wge(sc, "t_lnq", b_tq + ch)
                        c.inc(sc.activation(out=sSq[:], in_=sR[:, ch * TPC:(ch + 1) * TPC],
                                            func=AF.Square), "s_lnq")

                @block.tensor
                def _(t):
                    c.fence_wait(t, fence)
                    for ch in range(NCH_D):
                        for hh in range(2):
                            mm = t.matmul(psSum[:, hh * 512:(hh + 1) * 512], sOnesC[:],
                                          sR[:, ch * TPC + hh * 512: ch * TPC + (hh + 1) * 512],
                                          start=(ch == 0), stop=(ch == NCH_D - 1))
                    c.inc(mm, "t_lns")
                    for ch in range(NCH_D):
                        c.wge(t, "s_lnq", b_ss + ch + 1)
                        for hh in range(2):
                            mm = t.matmul(psSq[:, hh * 512:(hh + 1) * 512], sOnesC[:],
                                          sSq[:, hh * 512:(hh + 1) * 512],
                                          start=(ch == 0), stop=(ch == NCH_D - 1))
                        c.inc(mm, "t_lnq")

                @block.vector
                def _(v):
                    c.fence_wait(v, fence)
                    c.wge(v, "t_lns", b_ts + 1)
                    c.wge(v, "t_lnq", b_tq + NCH_D)
                    v.tensor_scalar(out=MU, in0=psSum[:, :], scalar1=1.0 / D, scalar2=None, op0=ALU.mult)
                    v.tensor_scalar(out=MSQ, in0=psSq[:, :], scalar1=1.0 / D, scalar2=None, op0=ALU.mult)
                    v.tensor_tensor(out=MU2, in0=MU, in1=MU, op=ALU.mult)
                    c.inc(v.tensor_tensor(out=VAR, in0=MSQ, in1=MU2, op=ALU.subtract), "v_lns")

                @block.scalar
                def _(sc):
                    c.wge(sc, "v_lns", b_vs + 1)
                    c.inc(sc.activation(out=SD, in_=VAR, func=AF.Sqrt, bias=sEps[0:1, :]), "s_lnd")

                @block.vector
                def _(v):
                    c.wge(v, "s_lnd", b_sd + 1)
                    v.reciprocal(out=RSTD, in_=SD)
                    v.scalar_tensor_tensor(out=BNEG, in0=MU, scalar=-1.0, in1=RSTD,
                                           op0=ALU.mult, op1=ALU.mult)
                    v.tensor_copy(out=sStatB[0:1, 0:TPC], in_=RSTD)
                    c.inc(v.tensor_copy(out=sStatB[0:1, TPC:2 * TPC], in_=BNEG), "v_lns")

                @block.tensor
                def _(t):
                    c.wge(t, "v_lns", b_vs + 2)
                    for q in range(4):
                        mm = t.matmul(psBrd[:, q * 512:(q + 1) * 512], sOnesR[:],
                                      sStatB[0:1, q * 512:(q + 1) * 512], start=True, stop=True)
                    c.inc(mm, "t_lnb")

                @block.vector
                def _(v):
                    c.wge(v, "t_lnb", b_tb + 1)
                    for ch in range(NCH_D):
                        v.tensor_tensor(out=sSq[:], in0=sR[:, ch * TPC:(ch + 1) * TPC],
                                        in1=psBrd[:, 0:TPC], op=ALU.mult)
                        last = v.tensor_tensor(out=sXn[:, ch * TPC:(ch + 1) * TPC], in0=sSq[:],
                                               in1=psBrd[:, TPC:2 * TPC], op=ALU.add)
                    c.inc(last, "v_ln")
                    c.inc(v.engine_nop(), "v_ms")
                c.expect("v_ln", b_vl + 1)

        def emit_ffn_a(first, sXn, fence):
            wsrc1 = dr["w1s"] if first else w1bf
            wsrc3 = dr["w3s"] if first else w3bf
            with ExitStack() as esf:
                tg = "a" if first else "b"
                sUGs = esf.enter_context(nc.sbuf_tensor("sUGs" + tg, [P, TPC], BF16))
                sHc = esf.enter_context(nc.sbuf_tensor("sHc" + tg, [P, TPC], BF16))
                sS1e = esf.enter_context(nc.sbuf_tensor("sS1e" + tg, [P, TPC], BF16))
                psA = esf.enter_context(nc.psum_tensor("psAa" if first else "psAb", [P, TPC], F32))
                psB = esf.enter_context(nc.psum_tensor("psBa" if first else "psBb", [P, TPC], F32))
                psC = esf.enter_context(nc.psum_tensor("psCa" if first else "psCb", [P, TPC], F32))
                b_lw = c.cnt("l_wf")
                b_gc = c.cnt("g_wc")
                b_ta = c.cnt("t_fa")
                b_ss = c.cnt("s_fsi")
                b_vm = c.cnt("v_fmu")
                b_t1 = c.cnt("t_fs1")
                b_se = c.cnt("s_fev")
                b_gs = c.cnt("g_fs1")
                b_gw = c.cnt("g_fwc")
                NU = 2 * NCH_H

                b_lwp = c.cnt("l_wp")

                @block.sync
                def _(s):
                    c.fence_wait(s, fence)
                    for u in range(NU):
                        m, hc = u % 2, u // 2
                        if first:
                            c.wge(s, "g_wc", b_gc + u - 1)
                            c.dma(s, sWf[u % 2][:, 0:1024],
                                  (wsrc1 if m == 0 else wsrc3)[hc, :, 0:1024], "l_wf")
                        else:
                            c.wge(s, "t_fa", b_ta + u - 1)
                            c.dma(s, sWb[u % 2][:, 0:1024],
                                  (wsrc1 if m == 0 else wsrc3)[hc, :, 0:1024], "l_wf")

                def emit_spill(g, hc):
                    c.wge(g, "s_fev", b_se + hc + 1)
                    c.dma(g, s1sp[hc, :, :], sS1e[:], "g_fs1")

                if first:
                    @block.gpsimd
                    def _(g):
                        c.fence_wait(g, fence)
                        for u in range(NU):
                            m, hc = u % 2, u // 2
                            c.wge(g, "g_wc", b_gc + u - 1)
                            c.dma(g, sWf[u % 2][:, 1024:2048],
                                  (wsrc1 if m == 0 else wsrc3)[hc, :, 1024:2048], "l_wp")
                            if m == 1 and hc >= 1:
                                emit_spill(g, hc - 1)
                            c.wge(g, "l_wf", b_lw + 16 * (u + 1))
                            c.wge(g, "l_wp", b_lwp + 16 * (u + 1))
                            c.wge(g, "t_fa", b_ta + u - 1)
                            c.wge(g, "g_fwc", b_gw + 16 * (u - 1))
                            c.inc(g.tensor_copy(out=sWb[u % 2][:], in_=sWf[u % 2][:]), "g_wc")
                            c.dma(g, (w1bf if m == 0 else w3bf)[hc, :, :], sWb[u % 2][:], "g_fwc")
                        emit_spill(g, NCH_H - 1)
                        c.inc(g.engine_nop(), "g_ms")
                else:
                    @block.gpsimd
                    def _(g):
                        c.fence_wait(g, fence)
                        for u in range(NU):
                            m, hc = u % 2, u // 2
                            if m == 1 and hc >= 1:
                                emit_spill(g, hc - 1)
                            c.wge(g, "t_fa", b_ta + u - 1)
                            c.dma(g, sWb[u % 2][:, 1024:2048],
                                  (wsrc1 if m == 0 else wsrc3)[hc, :, 1024:2048], "l_wp")
                        emit_spill(g, NCH_H - 1)
                        c.inc(g.engine_nop(), "g_ms")

                @block.tensor
                def _(t):
                    c.fence_wait(t, fence)
                    for hc in range(NCH_H):
                        for m in range(2):
                            u = 2 * hc + m
                            if first:
                                c.wge(t, "g_wc", b_gc + u + 1)
                            else:
                                c.wge(t, "l_wf", b_lw + 16 * (u + 1))
                                c.wge(t, "l_wp", b_lwp + 16 * (u + 1))
                            if m == 0:
                                c.wge(t, "s_fsi", b_ss + hc)
                            else:
                                c.wge(t, "v_fmu", b_vm + hc)
                            ps = psA if m == 0 else psB
                            for kc in range(NCH_D):
                                for q in range(2):
                                    mm = t.matmul(ps[:, q * 512:(q + 1) * 512],
                                                  sWb[u % 2][:, kc * P:(kc + 1) * P],
                                                  sXn[:, kc * TPC + q * 512: kc * TPC + (q + 1) * 512],
                                                  start=(kc == 0), stop=(kc == NCH_D - 1))
                            c.inc(mm, "t_fa")
                        c.wge(t, "v_fmu", b_vm + hc + 1)
                        c.wge(t, "s_fev", b_se + hc)
                        for q in range(2):
                            mm = t.matmul(psC[:, q * 512:(q + 1) * 512], sH128[:],
                                          sHc[:, q * 512:(q + 1) * 512], start=True, stop=True)
                        c.inc(mm, "t_fs1")

                @block.scalar
                def _(sc):
                    c.fence_wait(sc, fence)
                    for hc in range(NCH_H):
                        c.wge(sc, "t_fa", b_ta + 2 * hc + 1)
                        c.wge(sc, "v_fmu", b_vm + hc)
                        c.inc(sc.activation(out=sUGs[:], in_=psA[:, :], func=AF.Silu), "s_fsi")
                        c.wge(sc, "t_fs1", b_t1 + hc + 1)
                        c.wge(sc, "g_fs1", b_gs + 16 * hc)
                        c.inc(sc.activation(out=sS1e[:], in_=psC[:, :], func=AF.Copy,
                                            scale=INV_SQRT_N), "s_fev")

                @block.vector
                def _(v):
                    c.fence_wait(v, fence)
                    for hc in range(NCH_H):
                        c.wge(v, "s_fsi", b_ss + hc + 1)
                        c.wge(v, "t_fa", b_ta + 2 * hc + 2)
                        c.wge(v, "t_fs1", b_t1 + hc)
                        c.inc(v.tensor_tensor(out=sHc[:], in0=sUGs[:], in1=psB[:, :], op=ALU.mult),
                              "v_fmu")
                    c.inc(v.engine_nop(), "v_ms")


                c.expect("t_fa", b_ta + NU)
                c.expect("t_fs1", b_t1 + NCH_H)
                c.expect("s_fsi", b_ss + NCH_H)
                c.expect("s_fev", b_se + NCH_H)
                c.expect("v_fmu", b_vm + NCH_H)
                c.expect("g_fs1", b_gs + 16 * NCH_H)
            return b_gs

        def emit_ffn_b(first, b_gs):
            with ExitStack() as esb:
                sS1tc = esb.enter_context(
                    nc.sbuf_tensor("sS1t" + ("a" if first else "b"), [P, NCH_H * 512], BF16))
                psO = [esb.enter_context(
                    nc.psum_tensor(("psOa" if first else "psOb") + str(j), [P, 512], F32))
                    for j in range(4)]
                b_l1 = c.cnt("l_s1")
                b_lw2 = c.cnt("l_w2")
                b_tb = c.cnt("t_fb")
                b_vb = c.cnt("v_fb")
                se_a_done = c.cnt("s_fev")
                vm_a_done = c.cnt("v_fmu")

                fence_b = c.snap()

                b_l1p = c.cnt("l_s1p")
                b_lw2p = c.cnt("l_w2p")
                def wsl(i, a=0, b=512):
                    ti, c0 = divmod(i % 4, 2)
                    return sWb[ti][:, c0 * 512 + a: c0 * 512 + b]

                @block.sync
                def _(s):
                    c.fence_wait(s, fence_b)
                    for tc in range(2):
                        if tc == 1:
                            c.wge(s, "t_fb", b_tb + 256)
                        for hc in range(0, NCH_H, 2):
                            c.wge(s, "g_fs1", b_gs + 16 * (hc + 1))
                            c.dma(s, sS1tc[:, hc * 512:(hc + 1) * 512],
                                  s1sp[hc, :, tc * 512:(tc + 1) * 512], "l_s1")
                        for k in range(256):
                            dg, cb = k // NCH_H, k % NCH_H
                            gk = tc * 256 + k
                            if gk % 2 == 0:
                                c.wge(s, "t_fb", b_tb + gk - 3)
                                c.dma(s, wsl(gk),
                                      w2pbf[cb * P:(cb + 1) * P, dg * 512:(dg + 1) * 512], "l_w2")

                @block.gpsimd
                def _(g):
                    c.fence_wait(g, fence_b)
                    for tc in range(2):
                        if tc == 1:
                            c.wge(g, "t_fb", b_tb + 256)
                        for hc in range(1, NCH_H, 2):
                            c.wge(g, "g_fs1", b_gs + 16 * (hc + 1))
                            c.dma(g, sS1tc[:, hc * 512:(hc + 1) * 512],
                                  s1sp[hc, :, tc * 512:(tc + 1) * 512], "l_s1p")
                        for k in range(256):
                            dg, cb = k // NCH_H, k % NCH_H
                            gk = tc * 256 + k
                            if gk % 2 == 1:
                                c.wge(g, "t_fb", b_tb + gk - 3)
                                c.dma(g, wsl(gk),
                                      w2pbf[cb * P:(cb + 1) * P, dg * 512:(dg + 1) * 512], "l_w2p")
                    c.inc(g.engine_nop(), "g_ms")

                @block.tensor
                def _(t):
                    c.wge(t, "s_fev", se_a_done)
                    c.wge(t, "v_fmu", vm_a_done)
                    for tc in range(2):
                        c.wge(t, "l_s1", b_l1 + 16 * (NCH_H // 2) * (tc + 1))
                        c.wge(t, "l_s1p", b_l1p + 16 * (NCH_H // 2) * (tc + 1))
                        for k in range(256):
                            dg, cb = k // NCH_H, k % NCH_H
                            gk = tc * 256 + k
                            if gk % 2 == 0:
                                c.wge(t, "l_w2", b_lw2 + 16 * (gk // 2 + 1))
                            else:
                                c.wge(t, "l_w2p", b_lw2p + 16 * (gk // 2 + 1))
                            if cb == 0:
                                c.wge(t, "v_fb", b_vb + tc * 4 + dg)
                            for j in range(4):
                                mm = t.matmul(psO[j][:, :], wsl(gk, j * P, (j + 1) * P),
                                              sS1tc[:, cb * 512:(cb + 1) * 512],
                                              start=(cb == 0), stop=(cb == NCH_H - 1))
                            c.inc(mm, "t_fb")

                @block.vector
                def _(v):
                    for tc in range(2):
                        for dg in range(4):
                            c.wge(v, "t_fb", b_tb + tc * 256 + (dg + 1) * NCH_H)
                            for j in range(4):
                                dc = dg * 4 + j
                                sl = sR[:, dc * TPC + tc * 512: dc * TPC + (tc + 1) * 512]
                                last = v.scalar_tensor_tensor(out=sl, in0=psO[j][:, :], scalar=0.5,
                                                              in1=sl, op0=ALU.mult, op1=ALU.add)
                            c.inc(last, "v_fb")
                    c.inc(v.engine_nop(), "v_ms")
                c.expect("t_fb", b_tb + 512)
                c.expect("v_fb", b_vb + 8)

        if _STOP_PHASE < 3:
            _emit_taps()
            return nc
        fence = c.snap()
        with ExitStack() as esx:
            sXn = esx.enter_context(nc.sbuf_tensor("sXn1", [P, NCH_D * TPC], BF16))
            emit_ln("1", sXn, fence)
            fence = c.snap()
            ffn1_bgs = emit_ffn_a(True, sXn, fence)
        if _STOP_PHASE < 4:
            _emit_taps()
            return nc
        emit_ffn_b(True, ffn1_bgs)

        if _STOP_PHASE < 5:
            _emit_taps()
            return nc
        fence = c.snap()
        with ExitStack() as esx2:
            sXn2 = esx2.enter_context(nc.sbuf_tensor("sXn2", [P, NCH_D * TPC], BF16))
            emit_ln("2", sXn2, fence)
            v_ln2 = c.cnt("v_ln")

            @block.gpsimd
            def _(g):
                c.wge(g, "v_ln", v_ln2)
                for ch in range(NCH_D):
                    c.dma(g, ag_in[ch * P:(ch + 1) * P, :], sXn2[:, ch * TPC:(ch + 1) * TPC], "g_ag")
                c.wge(g, "g_ag", c.cnt("g_ag"))
                cc_i = g.collective_compute(
                    "AllGather", ALU.bypass,
                    replica_groups=[list(range(NCORES))],
                    ins=[ag_in.ap().opt()], outs=[ag_out.ap().opt()],
                )
                cc_i.then_inc(c.sem("cag"))
                c.n["cag"] += 1
                c.inc(g.engine_nop(), "g_ms")

        if _STOP_PHASE < 6:
            _emit_taps()
            return nc
        with ExitStack() as esm:
            if True:
                sRhs = esm.enter_context(nc.sbuf_tensor("sRhs", [P, NCH_D * TPC], BF16))
                sU4 = esm.enter_context(nc.sbuf_tensor("sU4", [P, NDT * TPC], BF16))
                sHt = esm.enter_context(nc.sbuf_tensor("sHt", [P, TPC + 4], F32))
                sCac = esm.enter_context(nc.sbuf_tensor("sCac", [P, TPC], F32))
                sTails = esm.enter_context(nc.sbuf_tensor("sTails", [P, NDT * 4], F32))
                sGe = esm.enter_context(nc.sbuf_tensor("sGe", [P, TPC], BF16))
                sXe = esm.enter_context(nc.sbuf_tensor("sXe", [P, TPC], F32))
                sXe2 = esm.enter_context(nc.sbuf_tensor("sXe2", [32, TPC], F32))
                psP = [esm.enter_context(nc.psum_tensor(f"psP{i}", [P, TPC], F32)) for i in range(2)]
                psX = esm.enter_context(nc.psum_tensor("psX", [P, TPC], F32))
                psX2 = esm.enter_context(nc.psum_tensor("psX2", [32, TPC], F32))
                b_lr = c.cnt("l_rhs")
                b_lw = c.cnt("l_wf")
                b_gc = c.cnt("g_wc")
                b_tp = c.cnt("t_pj")
                b_tx = c.cnt("t_wx")
                b_vc = c.cnt("v_cv")
                b_su = c.cnt("s_u")
                b_sg = c.cnt("s_g")
                b_vx = c.cnt("v_xe")
                b_gu = c.cnt("g_u")
                b_gg = c.cnt("g_g")
                b_gx = c.cnt("g_x")
                vfb_done = c.cnt("v_fb")
                if _DEBUG_TAPS:
                    c.dbg_sb += [("sXe", sXe, [P, TPC], F32), ("sXe2", sXe2, [32, TPC], F32),
                                 ("sU4", sU4, [P, NDT * TPC], BF16), ("sWxb", sWxb, [P, NDT * 160], BF16),
                                 ("sTails", sTails, [P, NDT * 4], F32), ("sHt", sHt, [P, TPC + 4], F32)]

                @block.vector
                def _(v):
                    c.inc(v.memset(sTails[:], 0.0), "v_tl")

                fence_fr = c.snap()

                b_lrs = c.cnt("l_rhs2")
                b_lwp = c.cnt("l_wp")

                @block.sync
                def _(s):
                    c.fence_wait(s, fence_fr)
                    s.wait_ge(c.sem("cag"), 1)
                    for cb in range(NCORES):
                        if cb >= 1:
                            c.wge(s, "t_pj", b_tp + 8 * (cb - 1) + 8)
                        for kc in range(0, NCH_D, 2):
                            c.dma(s, sRhs[:, kc * TPC:(kc + 1) * TPC],
                                  ag_out[cb * D + kc * P: cb * D + (kc + 1) * P, :], "l_rhs")
                        for oc in range(8):
                            u = cb * 8 + oc
                            c.wge(s, "g_wc", b_gc + u - 1)
                            c.dma(s, sWf[u % 2][:, 0:1024], dr["wins"][oc, :, 0:1024], "l_wf")

                def emit_front_dmas(g, cb):
                    for dt in range(NDT):
                        idx = cb * 4 + dt
                        c.wge(g, "s_u", b_su + idx + 1)
                        c.dma(g, usp[dt * P:(dt + 1) * P, cb * TPC:(cb + 1) * TPC],
                              sU4[:, dt * TPC:(dt + 1) * TPC], "g_u")
                    for oc in range(4, 8):
                        gidx = cb * 4 + (oc - 4)
                        dtg = oc - 4
                        c.wge(g, "s_g", b_sg + gidx + 1)
                        c.dma(g, gsp[dtg * P:(dtg + 1) * P, cb * TPC:(cb + 1) * TPC], sGe[:], "g_g")
                    c.wge(g, "v_xe", b_vx + 2 * (cb + 1))
                    c.dma(g, ar_in[0:P, cb * TPC:(cb + 1) * TPC], sXe[:], "g_x")
                    c.dma(g, ar_in[P:160, cb * TPC:(cb + 1) * TPC], sXe2[:], "g_x")

                @block.gpsimd
                def _(g):
                    for cb in range(NCORES):
                        if cb >= 1:
                            emit_front_dmas(g, cb - 1)
                        for oc in range(8):
                            u = cb * 8 + oc
                            c.wge(g, "g_wc", b_gc + u - 1)
                            c.dma(g, sWf[u % 2][:, 1024:2048], dr["wins"][oc, :, 1024:2048], "l_wp")
                            c.wge(g, "l_wf", b_lw + 16 * (u + 1))
                            c.wge(g, "l_wp", b_lwp + 16 * (u + 1))
                            c.wge(g, "t_pj", b_tp + u - 1)
                            c.inc(g.tensor_copy(out=sWb[u % 2][:], in_=sWf[u % 2][:]), "g_wc")
                    emit_front_dmas(g, NCORES - 1)
                    c.wge(g, "g_x", b_gx + 16 * 16)
                    cc_i = g.collective_compute(
                        "AllReduce", ALU.add,
                        replica_groups=[list(range(NCORES))],
                        ins=[ar_in.ap().opt()], outs=[ar_out.ap().opt()],
                    )
                    cc_i.then_inc(c.sem("car"))
                    c.n["car"] += 1
                    c.inc(g.engine_nop(), "g_ms")

                @block.tensor
                def _(t):
                    c.wge(t, "v_fb", vfb_done)
                    c.wge(t, "v_tl", 1)
                    for cb in range(NCORES):
                        c.wge(t, "l_rhs", b_lr + 16 * (NCH_D // 2) * (cb + 1))
                        c.wge(t, "l_rhs2", b_lrs + 16 * (NCH_D // 2) * (cb + 1))
                        for oc in range(8):
                            u = cb * 8 + oc
                            c.wge(t, "g_wc", b_gc + u + 1)
                            up = u - 2
                            if up >= 0:
                                ocp, cbp = up % 8, up // 8
                                if ocp < 4:
                                    c.wge(t, "v_cv", b_vc + cbp * 4 + ocp + 1)
                                else:
                                    c.wge(t, "s_g", b_sg + cbp * 4 + (ocp - 4) + 1)
                            for kc in range(NCH_D):
                                for q in range(2):
                                    mm = t.matmul(psP[u % 2][:, q * 512:(q + 1) * 512],
                                                  sWb[u % 2][:, kc * P:(kc + 1) * P],
                                                  sRhs[:, kc * TPC + q * 512: kc * TPC + (q + 1) * 512],
                                                  start=(kc == 0), stop=(kc == NCH_D - 1))
                            c.inc(mm, "t_pj")
                        c.wge(t, "s_u", b_su + cb * 4 + 4)
                        c.wge(t, "v_xe", b_vx + 2 * cb)
                        for j in range(2):
                            w = P if j == 0 else 32
                            ps = psX if j == 0 else psX2
                            for dt in range(NDT):
                                for q in range(2):
                                    mm = t.matmul(ps[:, q * 512:(q + 1) * 512],
                                                  sWxb[:, dt * 160 + j * P: dt * 160 + j * P + w],
                                                  sU4[:, dt * TPC + q * 512: dt * TPC + (q + 1) * 512],
                                                  start=(dt == 0), stop=(dt == NDT - 1))
                        c.inc(mm, "t_wx")

                @block.vector
                def _(v):
                    for cb in range(NCORES):
                        for dt in range(NDT):
                            u = cb * 8 + dt
                            idx = cb * 4 + dt
                            c.wge(v, "t_pj", b_tp + u + 1)
                            c.wge(v, "s_u", b_su + idx)
                            v.tensor_copy(out=sHt[:, 0:3], in_=sTails[:, dt * 4:dt * 4 + 3])
                            v.tensor_copy(out=sTails[:, dt * 4:dt * 4 + 3],
                                          in_=psP[u % 2][:, TPC - 3:TPC])
                            v.tensor_copy(out=sHt[:, 3:3 + TPC], in_=psP[u % 2][:, :])
                            v.tensor_scalar(out=sCac[:], in0=sHt[:, 0:TPC],
                                            scalar1=sCw[:, dt * 4:dt * 4 + 1], scalar2=None, op0=ALU.mult)
                            for k in range(1, 4):
                                last = v.scalar_tensor_tensor(out=sCac[:], in0=sHt[:, k:k + TPC],
                                                              scalar=sCw[:, dt * 4 + k:dt * 4 + k + 1],
                                                              in1=sCac[:], op0=ALU.mult, op1=ALU.add)
                            c.inc(last, "v_cv")
                        c.wge(v, "t_wx", b_tx + cb + 1)
                        c.wge(v, "g_x", b_gx + 16 * 2 * cb)
                        v.tensor_copy(out=sXe[:], in_=psX[:, :])
                        c.inc(v.tensor_copy(out=sXe2[:], in_=psX2[:, :]), "v_xe", 2)
                    c.inc(v.engine_nop(), "v_ms")

                @block.scalar
                def _(sc):
                    sc.wait_ge(c.sem("cag"), 1)
                    for cb in range(NCORES):
                        if cb >= 1:
                            c.wge(sc, "t_pj", b_tp + 8 * (cb - 1) + 8)
                        for kc in range(1, NCH_D, 2):
                            c.dma(sc, sRhs[:, kc * TPC:(kc + 1) * TPC],
                                  ag_out[cb * D + kc * P: cb * D + (kc + 1) * P, :], "l_rhs2")
                        for dt in range(NDT):
                            idx = cb * 4 + dt
                            c.wge(sc, "v_cv", b_vc + idx + 1)
                            c.wge(sc, "t_wx", b_tx + cb)
                            if cb >= 1:
                                c.wge(sc, "g_u", b_gu + 16 * (4 * (cb - 1) + dt + 1))
                            c.inc(sc.activation(out=sU4[:, dt * TPC:(dt + 1) * TPC], in_=sCac[:],
                                                func=AF.Silu, bias=sCb[:, dt:dt + 1]), "s_u")
                        for oc in range(4, 8):
                            u = cb * 8 + oc
                            gidx = cb * 4 + (oc - 4)
                            c.wge(sc, "t_pj", b_tp + u + 1)
                            c.wge(sc, "g_g", b_gg + 16 * gidx)
                            c.inc(sc.activation(out=sGe[:], in_=psP[u % 2][:, :], func=AF.Copy), "s_g")


                c.expect("t_pj", b_tp + 64)
                c.expect("v_cv", b_vc + 32)
                c.expect("s_u", b_su + 32)
                c.expect("s_g", b_sg + 32)
                c.expect("v_xe", b_vx + 16)
                c.expect("g_u", b_gu + 16 * 32)
                c.expect("g_g", b_gg + 16 * 32)
                c.expect("g_x", b_gx + 16 * 16)

        if _STOP_PHASE < 7:
            _emit_taps()
            return nc
        sTs = E(nc.sbuf_tensor("sTs", [P, L], BF16))
        with ExitStack() as esmid:
            sStg = esmid.enter_context(nc.sbuf_tensor("sStg", [P, 4096], F32))
            b_lt = c.cnt("l_ts")
            b_vt = c.cnt("v_ts")

            fence_mid = c.snap()

            @block.sync
            def _(s):
                c.fence_wait(s, fence_mid)
                s.wait_ge(c.sem("car"), 1)
                for i in range(2):
                    c.wge(s, "v_ts", b_vt + i)
                    c.dma(s, sStg[:], ar_out[0:P, i * 4096:(i + 1) * 4096], "l_ts")

            @block.vector
            def _(v):
                for i in range(2):
                    c.wge(v, "l_ts", b_lt + 16 * (i + 1))
                    c.inc(v.tensor_copy(out=sTs[:, i * 4096:(i + 1) * 4096], in_=sStg[:]), "v_ts")
                c.inc(v.engine_nop(), "v_ms")
            c.expect("v_ts", b_vt + 2)

        if _STOP_PHASE < 8:
            _emit_taps()
            return nc
        fence = c.snap()
        with ExitStack() as ess:
            sBCs = ess.enter_context(nc.sbuf_tensor("sBCs", [32, TSC], F32))
            sBCt = ess.enter_context(nc.sbuf_tensor("sBCt", [32, TSC], BF16))
            sBbc = ess.enter_context(nc.sbuf_tensor("sBbc", [P, NS * TSC], BF16))
            sCbc = ess.enter_context(nc.sbuf_tensor("sCbc", [P, NS * TSC], BF16))
            sDtSl = ess.enter_context(nc.sbuf_tensor("sDtSl", [P, TSC], F32))
            sT1 = ess.enter_context(nc.sbuf_tensor("sT1", [P, TSC], F32))
            sT2 = ess.enter_context(nc.sbuf_tensor("sT2", [P, TSC], F32))
            sDtu = ess.enter_context(nc.sbuf_tensor("sDtu", [P, TSC], BF16))
            sdBu = ess.enter_context(nc.sbuf_tensor("sdBu", [P, TSC], BF16))
            sdA = [ess.enter_context(nc.sbuf_tensor(f"sdA{i}", [P, TSC], F32)) for i in range(2)]
            sSoA = ess.enter_context(nc.sbuf_tensor("sSoA", [P, NS * TSC], BF16))
            sSC = ess.enter_context(nc.sbuf_tensor("sSC", [P, TSC], BF16))
            sUt = [ess.enter_context(nc.sbuf_tensor(f"sUt{i}", [P, TSC], BF16)) for i in range(2)]
            sGt = [ess.enter_context(nc.sbuf_tensor(f"sGt{i}", [P, TSC], BF16)) for i in range(2)]
            sSg = ess.enter_context(nc.sbuf_tensor("sSg", [P, TSC], F32))
            sY1 = ess.enter_context(nc.sbuf_tensor("sY1", [P, TSC], F32))
            sYp = ess.enter_context(nc.sbuf_tensor("sYp", [P, TSC], BF16))
            psBc = [ess.enter_context(nc.psum_tensor(f"psBc{i}", [P, TSC], F32)) for i in range(2)]
            psDt = ess.enter_context(nc.psum_tensor("psDt", [P, TSC], F32))
            psY = [ess.enter_context(nc.psum_tensor(f"psYs{i}", [P, TSC], F32)) for i in range(2)]
            b_lu = c.cnt("l_ug")
            b_lbc = c.cnt("l_bc")
            b_vcc = c.cnt("v_bcc")
            b_tb = c.cnt("t_bc")
            b_vb = c.cnt("v_bc")
            b_td = c.cnt("t_dt")
            b_sp = c.cnt("s_sp")
            b_vd = c.cnt("v_dts")
            b_sa = c.cnt("s_dA")
            b_vn = c.cnt("v_scn")
            b_vq = c.cnt("v_sc")
            b_tr = c.cnt("t_rd")
            b_vs = c.cnt("v_stt")
            b_vy = c.cnt("v_yp")
            b_ssg = c.cnt("s_sg")
            b_gy = c.cnt("g_yp")

            b_lgs = c.cnt("l_gs")

            @block.sync
            def _(s):
                c.fence_wait(s, fence)
                for k in range(NTCH * NDT):
                    tch, dt = k // NDT, k % NDT
                    if dt == 0:
                        c.wge(s, "v_bcc", b_vcc + tch)
                        c.dma(s, sBCs[:], ar_out[P:160, tch * TSC:(tch + 1) * TSC], "l_bc")
                    c.wge(s, "v_stt", b_vs + k - 1)
                    c.dma(s, sUt[k % 2][:], usp[dt * P:(dt + 1) * P, tch * TSC:(tch + 1) * TSC], "l_ug")

            @block.tensor
            def _(t):
                c.fence_wait(t, fence)
                for tch in range(NTCH):
                    c.wge(t, "v_bcc", b_vcc + tch + 1)
                    for n in range(2 * NS):
                        bn = tch * 2 * NS + n
                        c.wge(t, "v_bc", b_vb + bn - 1)
                        mm = t.matmul(psBc[bn % 2][:, :], sEsel[:, n * P:(n + 1) * P],
                                      sBCt[:, :], start=True, stop=True)
                        c.inc(mm, "t_bc")
                    for dt in range(NDT):
                        k = tch * NDT + dt
                        c.wge(t, "s_sp", b_sp + k)
                        mm = t.matmul(psDt[:, :], sWdtb[:, dt * P:(dt + 1) * P],
                                      sTs[:, tch * TSC:(tch + 1) * TSC], start=True, stop=True)
                        c.inc(mm, "t_dt")
                        c.wge(t, "v_stt", b_vs + k - 1)
                        for n in range(NS):
                            c.wge(t, "v_sc", b_vq + k * NS + n + 1)
                            mm = t.matmul(psY[k % 2][:, :], sEyeB[:], sSC[:, :],
                                          start=(n == 0), stop=(n == NS - 1))
                            c.inc(mm, "t_rd")

            @block.scalar
            def _(sc):
                c.fence_wait(sc, fence)
                for tch in range(NTCH):
                    for dt in range(NDT):
                        k = tch * NDT + dt
                        c.wge(sc, "v_yp", b_vy + k - 1)
                        c.dma(sc, sGt[k % 2][:], gsp[dt * P:(dt + 1) * P, tch * TSC:(tch + 1) * TSC], "l_gs")
                        c.wge(sc, "t_dt", b_td + k + 1)
                        c.wge(sc, "v_dts", b_vd + k)
                        sc.activation(out=sT1[:], in_=psDt[:, :], func=AF.Abs, bias=sBdt[:, dt:dt + 1])
                        sc.activation(out=sT2[:], in_=sT1[:], func=AF.Exp, scale=-1.0)
                        sc.activation(out=sT2[:], in_=sT2[:], func=AF.Ln, bias=1.0)
                        c.inc(sc.activation(out=sT1[:], in_=psDt[:, :], func=AF.Relu,
                                            bias=sBdt[:, dt:dt + 1]), "s_sp")
                        c.wge(sc, "v_dts", b_vd + k + 1)
                        for n in range(NS):
                            if n >= 2:
                                c.wge(sc, "v_scn", b_vn + k * NS + n - 1)
                            c.inc(sc.activation(out=sdA[n % 2][:], in_=sDtSl[:], func=AF.Exp,
                                                scale=sA[:, dt * NS + n: dt * NS + n + 1]), "s_dA")
                        c.wge(sc, "v_yp", b_vy + k)
                        c.inc(sc.activation(out=sSg[:], in_=sGt[k % 2][:], func=AF.Silu), "s_sg")

            @block.vector
            def _(v):
                c.fence_wait(v, fence)
                for tch in range(NTCH):
                    c.wge(v, "l_bc", b_lbc + 16 * (tch + 1))
                    c.wge(v, "t_bc", b_tb + 32 * tch)
                    c.inc(v.tensor_copy(out=sBCt[:], in_=sBCs[:]), "v_bcc")
                    for n in range(2 * NS):
                        bn = tch * 2 * NS + n
                        c.wge(v, "t_bc", b_tb + bn + 1)
                        dst = sBbc if n < NS else sCbc
                        nn = n % NS
                        c.inc(v.tensor_copy(out=dst[:, nn * TSC:(nn + 1) * TSC],
                                            in_=psBc[bn % 2][:, :]), "v_bc")
                    for dt in range(NDT):
                        k = tch * NDT + dt
                        c.wge(v, "s_sp", b_sp + k + 1)
                        c.wge(v, "l_ug", b_lu + 16 * (k + 1))
                        v.tensor_add(out=sDtSl[:], in0=sT1[:], in1=sT2[:])
                        c.inc(v.tensor_tensor(out=sDtu[:], in0=sDtSl[:], in1=sUt[k % 2][:],
                                              op=ALU.mult), "v_dts")
                        for n in range(NS):
                            v.tensor_tensor(out=sdBu[:], in0=sDtu[:],
                                            in1=sBbc[:, n * TSC:(n + 1) * TSC], op=ALU.mult)
                            c.wge(v, "s_dA", b_sa + k * NS + n + 1)
                            c.inc(v.tensor_tensor_scan(
                                out=sSoA[:, n * TSC:(n + 1) * TSC], data0=sdA[n % 2][:],
                                data1=sdBu[:], initial=sState[:, dt * NS + n: dt * NS + n + 1],
                                op0=ALU.mult, op1=ALU.add), "v_scn")
                            c.wge(v, "t_rd", b_tr + k * NS + n)
                            c.inc(v.tensor_tensor(out=sSC[:], in0=sSoA[:, n * TSC:(n + 1) * TSC],
                                                  in1=sCbc[:, n * TSC:(n + 1) * TSC], op=ALU.mult),
                                  "v_sc")
                        c.wge(v, "t_rd", b_tr + (k + 1) * NS)
                        c.inc(v.scalar_tensor_tensor(out=sY1[:], in0=sUt[k % 2][:],
                                                     scalar=sDp[:, dt:dt + 1], in1=psY[k % 2][:, :],
                                                     op0=ALU.mult, op1=ALU.add), "v_stt")
                        src = sSoA[:].rearrange("p (n t) -> p n t", n=NS)[:, :, TSC - 1:TSC]
                        v.tensor_copy(out=sState[:, dt * NS:(dt + 1) * NS], in_=src)
                        c.wge(v, "s_sg", b_ssg + k + 1)
                        c.wge(v, "g_yp", b_gy + 16 * k)
                        c.inc(v.tensor_tensor(out=sYp[:], in0=sY1[:], in1=sSg[:], op=ALU.mult), "v_yp")
                c.inc(v.engine_nop(), "v_ms")

            @block.gpsimd
            def _(g):
                for k in range(NTCH * NDT):
                    tch, dt = k // NDT, k % NDT
                    c.wge(g, "v_yp", b_vy + k + 1)
                    dst_core = tch // 2
                    coff = (tch % 2) * TSC
                    c.dma(g, a2a_in[dst_core * DSC + dt * P: dst_core * DSC + (dt + 1) * P,
                                    coff:coff + TSC], sYp[:], "g_yp")
                c.wge(g, "g_yp", b_gy + 16 * NTCH * NDT)
                with allow_ici_all_to_all():
                    cc_i = g.collective_compute(
                        "AllToAll", ALU.bypass,
                        replica_groups=[list(range(NCORES))],
                        ins=[a2a_in.ap().opt()], outs=[a2a_out.ap().opt()],
                    )
                cc_i.then_inc(c.sem("ca2a"))
                c.n["ca2a"] += 1
                c.inc(g.engine_nop(), "g_ms")
            c.expect("t_bc", b_tb + NTCH * 2 * NS)
            c.expect("v_bc", b_vb + NTCH * 2 * NS)
            c.expect("t_rd", b_tr + NTCH * NDT * NS)
            c.expect("v_scn", b_vn + NTCH * NDT * NS)
            c.expect("v_sc", b_vq + NTCH * NDT * NS)
            c.expect("v_stt", b_vs + NTCH * NDT)
            c.expect("v_yp", b_vy + NTCH * NDT)
            c.expect("s_dA", b_sa + NTCH * NDT * NS)
            c.expect("s_sg", b_ssg + NTCH * NDT)
            c.expect("g_yp", b_gy + 16 * NTCH * NDT)

        if _STOP_PHASE < 9:
            _emit_taps()
            return nc
        fence = c.snap()
        with ExitStack() as eso:
            sYr = [eso.enter_context(nc.sbuf_tensor(f"sYr{i}", [P, 512], BF16)) for i in range(2)]
            psW = [eso.enter_context(nc.psum_tensor(f"psWo{j}", [P, 512], F32)) for j in range(4)]
            b_lw = c.cnt("l_wf")
            b_lr = c.cnt("l_yr")
            b_gc = c.cnt("g_wc")
            b_tw = c.cnt("t_wo")
            b_vw = c.cnt("v_wo")
            NU = 256

            @block.sync
            def _(s):
                c.fence_wait(s, fence)
                s.wait_ge(c.sem("ca2a"), 1)
                for u in range(NU):
                    tc2, dg, kc = u // 128, (u // 32) % 4, u % 32
                    c.wge(s, "g_wc", b_gc + u - 1)
                    c.dma(s, sWf[u % 2][:, 0:512], dr["wouts"][dg, kc, :, :], "l_wf")
                    c.wge(s, "t_wo", b_tw + u - 1)
                    c.dma(s, sYr[u % 2][:], a2a_out[kc * P:(kc + 1) * P, tc2 * 512:(tc2 + 1) * 512],
                          "l_yr")

            @block.gpsimd
            def _(g):
                c.fence_wait(g, fence)
                for u in range(NU):
                    tc2, dg, kc = u // 128, (u // 32) % 4, u % 32
                    c.wge(g, "g_wc", b_gc + u - 1)
                    c.dma(g, sWf[u % 2][:, 256:512], dr["wouts"][dg, kc, :, 256:512], "l_wp")
                    c.wge(g, "l_wf", b_lw + 16 * (u + 1))
                    c.wge(g, "l_wp", b_lwp + 16 * (u + 1))
                    c.wge(g, "t_wo", b_tw + u - 1)
                    c.inc(g.tensor_copy(out=sWb[u % 2][:, 0:512], in_=sWf[u % 2][:, 0:512]), "g_wc")
                c.inc(g.engine_nop(), "g_ms")

            @block.tensor
            def _(t):
                c.fence_wait(t, fence)
                for u in range(NU):
                    tc2, dg, kc = u // 128, (u // 32) % 4, u % 32
                    grp = u // 32
                    c.wge(t, "g_wc", b_gc + u + 1)
                    c.wge(t, "l_yr", b_lr + 16 * (u + 1))
                    if kc == 0 and grp >= 1:
                        c.wge(t, "v_wo", b_vw + grp)
                    for j in range(4):
                        mm = t.matmul(psW[j][:, :], sWb[u % 2][:, j * P:(j + 1) * P], sYr[u % 2][:],
                                      start=(kc == 0), stop=(kc == 31))
                    c.inc(mm, "t_wo")

            @block.vector
            def _(v):
                c.fence_wait(v, fence)
                for grp in range(8):
                    tc2, dg = grp // 4, grp % 4
                    c.wge(v, "t_wo", b_tw + (grp + 1) * 32)
                    for j in range(4):
                        dc = dg * 4 + j
                        sl = sR[:, dc * TPC + tc2 * 512: dc * TPC + (tc2 + 1) * 512]
                        last = v.scalar_tensor_tensor(out=sl, in0=psW[j][:, :], scalar=1.0, in1=sl,
                                                      op0=ALU.mult, op1=ALU.add)
                    c.inc(last, "v_wo")
                c.inc(v.engine_nop(), "v_ms")
            c.expect("t_wo", b_tw + NU)
            c.expect("v_wo", b_vw + 8)

        if _STOP_PHASE < 10:
            _emit_taps()
            return nc
        fence = c.snap()
        with ExitStack() as esx:
            sXn = esx.enter_context(nc.sbuf_tensor("sXn3", [P, NCH_D * TPC], BF16))
            emit_ln("3", sXn, fence)
            fence = c.snap()
            ffn2_bgs = emit_ffn_a(False, sXn, fence)
        emit_ffn_b(False, ffn2_bgs)

        if _STOP_PHASE < 11:
            _emit_taps()
            return nc
        fence = c.snap()
        with ExitStack() as eso:
            sOut = eso.enter_context(nc.sbuf_tensor("sOut", [P, D], F32))
            psT = eso.enter_context(nc.psum_tensor("psT1", [P, D], F32))
            b_to = c.cnt("t_ot")
            b_vo = c.cnt("v_ot")
            b_go = c.cnt("g_ot")

            @block.tensor
            def _(t):
                c.fence_wait(t, fence)
                for tt in range(8):
                    c.wge(t, "v_ot", b_vo + tt)
                    for ch in range(NCH_D):
                        mm = t.transpose(psT[:, ch * P:(ch + 1) * P],
                                         sR[:, ch * TPC + tt * P: ch * TPC + (tt + 1) * P], sEyeF[:])
                    c.inc(mm, "t_ot")

            @block.vector
            def _(v):
                c.fence_wait(v, fence)
                for tt in range(8):
                    c.wge(v, "t_ot", b_to + tt + 1)
                    c.wge(v, "g_ot", b_go + 16 * tt)
                    c.inc(v.tensor_copy(out=sOut[:], in_=psT[:, :]), "v_ot")

            @block.gpsimd
            def _(g):
                for tt in range(8):
                    c.wge(g, "v_ot", b_vo + tt + 1)
                    c.dma(g, out_t[tt * P:(tt + 1) * P, :], sOut[:], "g_ot")
                c.wge(g, "g_ot", b_go + 16 * 8)
            c.expect("t_ot", b_to + 8)
            c.expect("v_ot", b_vo + 8)
            c.expect("g_ot", b_go + 16 * 8)

        _emit_taps()

    return nc


def _host_inputs(inputs):
    x = np.asarray(inputs["L0_attn_hidden_states"])[0]
    w_in = np.asarray(inputs["L0_attn_W_in"])
    conv_w = np.asarray(inputs["L0_attn_conv_w"])[:, 0, :]
    conv_b = np.asarray(inputs["L0_attn_conv_b"])
    w_x = np.asarray(inputs["L0_attn_W_x"])
    w_dt = np.asarray(inputs["L0_attn_W_dt"])
    b_dt = np.asarray(inputs["L0_attn_b_dt"])
    a_log = np.asarray(inputs["L0_attn_A_log"])
    d_par = np.asarray(inputs["L0_attn_D_param"])
    w_out = np.asarray(inputs["L0_attn_W_out"])
    w1 = np.asarray(inputs["L0_ffn_W1"])
    w2 = np.asarray(inputs["L0_ffn_W2"])
    w3 = np.asarray(inputs["L0_ffn_W3"])

    def stream_oi(wT):
        K, O = wT.shape
        nkc, noc = K // P, O // P
        return np.ascontiguousarray(
            wT.reshape(nkc, P, noc, P).transpose(2, 1, 0, 3).reshape(noc, P, nkc * P))

    w1s = stream_oi(np.ascontiguousarray(w1.T))
    w3s = stream_oi(np.ascontiguousarray(w3.T))
    w2T = np.ascontiguousarray(w2.T)
    w2g = np.ascontiguousarray(w2T.reshape(64, P, D).transpose(1, 0, 2).reshape(HF, D))
    woutT = np.ascontiguousarray(w_out.T)
    wouts = np.ascontiguousarray(woutT.reshape(32, P, 4, 512).transpose(2, 0, 1, 3))

    h64 = _had(64)
    h128 = _had(128).astype(BF)
    bd64 = np.zeros((P, P), np.float32)
    bd64[0:64, 0:64] = h64
    bd64[64:128, 64:128] = h64
    bd64 = bd64.astype(BF)
    eyef = np.eye(P, dtype=np.float32)
    eyeb = eyef.astype(BF)
    onesr = np.ones((1, P), np.float32).astype(BF)
    onesc = np.ones((P, 1), np.float32)
    esel = np.zeros((32, 32 * P), np.float32)
    for n in range(NS):
        esel[n, n * P:(n + 1) * P] = 1.0
        esel[NS + n, (NS + n) * P:(NS + n + 1) * P] = 1.0
    esel = esel.astype(BF)

    winT = np.ascontiguousarray(w_in.T)
    in_maps = []
    for cc in range(NCORES):
        ds = slice(cc * DSC, (cc + 1) * DSC)
        win_slice = np.concatenate(
            [winT[:, cc * DSC:(cc + 1) * DSC], winT[:, DM + cc * DSC: DM + (cc + 1) * DSC]], axis=1)
        wins = np.ascontiguousarray(
            win_slice.reshape(NCH_D, P, 8, P).transpose(2, 1, 0, 3).reshape(8, P, NCH_D * P))
        in_maps.append(dict(
            x_t=np.ascontiguousarray(x[cc * TPC:(cc + 1) * TPC, :]),
            w1s=w1s, w3s=w3s, w2g=w2g, wouts=wouts, wins=wins,
            wxs=np.ascontiguousarray(w_x.T[ds, :].reshape(NDT, P, 160).transpose(1, 0, 2).reshape(P, -1)),
            wdts=np.ascontiguousarray(w_dt.T[:, ds]),
            cws=np.ascontiguousarray(conv_w[ds, :].reshape(NDT, P, 4).transpose(1, 0, 2).reshape(P, -1)),
            cbs=np.ascontiguousarray(conv_b[ds].reshape(NDT, P).T),
            bdts=np.ascontiguousarray(b_dt[ds].reshape(NDT, P).T),
            alogs=np.ascontiguousarray(a_log[ds, :].reshape(NDT, P, NS).transpose(1, 0, 2).reshape(P, -1)),
            dpars=np.ascontiguousarray(d_par[ds].reshape(NDT, P).T),
            h128=h128, bd64=bd64, eyef=eyef, eyeb=eyeb, onesr=onesr, onesc=onesc,
            esel=esel,
        ))
    return in_maps


def kernel(**inputs):
    in_maps = _host_inputs(inputs)
    nc = build_program()
    res = run_bass_kernel_spmd(nc, in_maps, core_ids=list(range(NCORES)))
    out = np.concatenate([res.results[cc]["out_t"] for cc in range(NCORES)], axis=0)
    return out.reshape(1, L, D).astype(np.float32)


if __name__ == "__main__":
    nc = build_program()
    print("built ok;", nc.n_instructions(), "instructions")


# version 10
# speedup vs baseline: 1.0890x; 1.0534x over previous
import numpy as np
import ml_dtypes
import concourse.bass as bass
import concourse.mybir as mybir
from concourse.bass_utils import run_bass_kernel_spmd
from concourse.aot_env import allow_ici_all_to_all
from contextlib import ExitStack

F32 = mybir.dt.float32
BF16 = mybir.dt.bfloat16
AF = mybir.ActivationFunctionType
ALU = mybir.AluOpType
BF = ml_dtypes.bfloat16

NCORES = 8
P = 128
L = 8192
D = 2048
HF = 8192
DM = 4096
NS = 16
RDT = 128
TPC = L // NCORES
DSC = DM // NCORES
NCH_D = D // P
NCH_H = HF // P
NDT = DSC // P
TSC = 512
NTCH = L // TSC
INV_SQRT_N = float(1.0 / np.sqrt(HF))

FENCE_SEMS = ["t_ms", "v_ms", "s_ms", "g_ms"]
_STOP_PHASE = 99
_DEBUG_TAPS = False


def _had(n):
    h = np.array([[1.0]], dtype=np.float32)
    while h.shape[0] < n:
        h = np.block([[h, h], [h, -h]])
    return h


class Ctx:
    def __init__(self, nc, es):
        self.nc = nc
        self.es = es
        self.h = {}
        self.n = {}

    def sem(self, name):
        if name not in self.h:
            self.h[name] = self.es.enter_context(self.nc.semaphore(name))
            self.n[name] = 0
        return self.h[name]

    def cnt(self, name):
        self.sem(name)
        return self.n[name]

    def inc(self, instr, name, amt=1):
        instr.then_inc(self.sem(name), amt)
        self.n[name] += amt
        return self.n[name]

    def dma(self, eng, out, in_, name):
        return self.inc(eng.dma_start(out=out, in_=in_), name, 16)

    def wge(self, eng, name, val):
        if val > 0:
            eng.wait_ge(self.sem(name), val)

    def snap(self):
        return dict(self.n)

    def fence_wait(self, eng, snap):
        for k, v in snap.items():
            if v > 0:
                eng.wait_ge(self.sem(k), v)

    def expect(self, name, val):
        assert self.n[name] == val, f"sem {name}: built {self.n[name]} != predicted {val}"


def build_program():
    nc = bass.Bass("TRN2", target_bir_lowering=False)
    dr = {}

    def din(name, shape, dt=F32):
        dr[name] = nc.dram_tensor(name, list(shape), dt, kind="ExternalInput")
        return dr[name]

    din("x_t", (TPC, D))
    din("w1s", (NCH_H, P, NCH_D * P))
    din("w3s", (NCH_H, P, NCH_D * P))
    din("w2g", (HF, D))
    din("wouts", (4, 32, P, 512))
    din("wins", (8, P, NCH_D * P))
    din("wxs", (P, NDT * 160))
    din("wdts", (RDT, DSC))
    din("cws", (P, NDT * 4))
    din("cbs", (P, NDT))
    din("bdts", (P, NDT))
    din("alogs", (P, NDT * NS))
    din("dpars", (P, NDT))
    din("h128", (P, P), BF16)
    din("bd64", (P, P), BF16)
    din("eyef", (P, P))
    din("eyeb", (P, P), BF16)
    din("onesr", (1, P), BF16)
    din("onesc", (P, 1))
    din("esel", (32, 32 * P), BF16)

    out_t = nc.dram_tensor("out_t", [TPC, D], F32, kind="ExternalOutput")
    if _DEBUG_TAPS:
        dbg = {nm: nc.dram_tensor("dbg_" + nm, shp, dt, kind="ExternalOutput")
               for nm, shp, dt in [
                   ("ag", [NCORES * D, TPC], BF16), ("ar", [160, L], F32),
                   ("a2a", [NCORES * DSC, TPC], BF16), ("usp", [DSC, L], BF16),
                   ("gsp", [DSC, L], BF16), ("w2p", [HF, D], BF16),
                   ("s1", [NCH_H, P, TPC], BF16), ("ari", [160, L], F32)]}

    w1bf = nc.dram_tensor("w1bf", [NCH_H, P, NCH_D * P], BF16)
    w3bf = nc.dram_tensor("w3bf", [NCH_H, P, NCH_D * P], BF16)
    w2pbf = nc.dram_tensor("w2pbf", [HF, D], BF16)
    s1sp = nc.dram_tensor("s1sp", [NCH_H, P, TPC], BF16)
    usp = nc.dram_tensor("usp", [DSC, L], BF16)
    gsp = nc.dram_tensor("gsp", [DSC, L], BF16)
    ag_in = nc.dram_tensor("ag_in", [D, TPC], BF16)
    ag_out = nc.dram_tensor("ag_out", [NCORES * D, TPC], BF16)
    ar_in = nc.dram_tensor("ar_in", [160, L], F32)
    ar_out = nc.dram_tensor("ar_out", [160, L], F32)
    a2a_in = nc.dram_tensor("a2a_in", [NCORES * DSC, TPC], BF16)
    a2a_out = nc.dram_tensor("a2a_out", [NCORES * DSC, TPC], BF16)

    es = ExitStack()
    with es:
        c = Ctx(nc, es)
        E = es.enter_context

        sR = E(nc.sbuf_tensor("sR", [P, NCH_D * TPC], F32))
        sH128 = E(nc.sbuf_tensor("sH128", [P, P], BF16))
        sBD64 = E(nc.sbuf_tensor("sBD64", [P, P], BF16))
        sEyeF = E(nc.sbuf_tensor("sEyeF", [P, P], F32))
        sEyeB = E(nc.sbuf_tensor("sEyeB", [P, P], BF16))
        sOnesR = E(nc.sbuf_tensor("sOnesR", [1, P], BF16))
        sOnesC = E(nc.sbuf_tensor("sOnesC", [P, 1], F32))
        sEsel = E(nc.sbuf_tensor("sEsel", [32, 32 * P], BF16))
        sA = E(nc.sbuf_tensor("sA", [P, NDT * NS], F32))
        sAtmp = E(nc.sbuf_tensor("sAtmp", [P, NDT * NS], F32))
        sCw = E(nc.sbuf_tensor("sCw", [P, NDT * 4], F32))
        sCb = E(nc.sbuf_tensor("sCb", [P, NDT], F32))
        sBdt = E(nc.sbuf_tensor("sBdt", [P, NDT], F32))
        sDp = E(nc.sbuf_tensor("sDp", [P, NDT], F32))
        sState = E(nc.sbuf_tensor("sState", [P, NDT * NS], F32))
        sEps = E(nc.sbuf_tensor("sEps", [P, 1], F32))
        sWdtb = E(nc.sbuf_tensor("sWdtb", [P, DSC], BF16))
        sWxb = E(nc.sbuf_tensor("sWxb", [P, NDT * 160], BF16))
        es0w = ExitStack()
        sWdtS = es0w.enter_context(nc.sbuf_tensor("sWdtS", [P, DSC], F32))
        sWxS = es0w.enter_context(nc.sbuf_tensor("sWxS", [P, NDT * 160], F32))

        block = E(nc.Block())

        c.dbg_sb = []

        def _emit_taps():
            if not _DEBUG_TAPS:
                return
            sems = list(c.h.items())
            counts = {k: c.n[k] for k, _ in sems}

            @block.gpsimd
            def _(g):
                for k, v in counts.items():
                    if v > 0 and not k.startswith("c"):
                        g.wait_ge(c.h[k], v)
                for nm, scr in [("ag", ag_out), ("ar", ar_out), ("a2a", a2a_out),
                                ("usp", usp), ("gsp", gsp), ("w2p", w2pbf), ("s1", s1sp),
                                ("ari", ar_in)]:
                    c.dma(g, dbg[nm].ap(), scr.ap(), "g_dbg")
                for nm, hnd, shp, dt in c.dbg_sb:
                    od = nc.dram_tensor("dbgsb_" + nm, list(shp), dt, kind="ExternalOutput")
                    c.dma(g, od.ap(), hnd[:], "g_dbg")
                c.wge(g, "g_dbg", c.cnt("g_dbg"))

        @block.sync
        def _(s):
            c.dma(s, sH128[:], dr["h128"][:, :], "q_ld")
            c.dma(s, sBD64[:], dr["bd64"][:, :], "q_ld")
            c.dma(s, sEyeF[:], dr["eyef"][:, :], "q_ld")
            c.dma(s, sEyeB[:], dr["eyeb"][:, :], "q_ld")
            c.dma(s, sOnesR[:], dr["onesr"][:, :], "q_ld")
            c.dma(s, sOnesC[:], dr["onesc"][:, :], "q_ld")
            c.dma(s, sEsel[:], dr["esel"][:, :], "q_ld")
            c.dma(s, sAtmp[:], dr["alogs"][:, :], "q_ld")
            c.dma(s, sCw[:], dr["cws"][:, :], "q_ld")
            c.dma(s, sCb[:], dr["cbs"][:, :], "q_ld")
            c.dma(s, sBdt[:], dr["bdts"][:, :], "q_ld")
            c.dma(s, sDp[:], dr["dpars"][:, :], "q_ld")
            c.dma(s, sWdtS[:], dr["wdts"][:, :], "q_ld")
            c.dma(s, sWxS[:], dr["wxs"][:, :], "q_ld")
        ld_consts = c.cnt("q_ld")

        @block.scalar
        def _(sc):
            c.wge(sc, "q_ld", ld_consts)
            c.inc(sc.activation(out=sA[:], in_=sAtmp[:], func=AF.Exp), "s_ms")
        s_exp = c.cnt("s_ms")

        @block.vector
        def _(v):
            c.wge(v, "s_ms", s_exp)
            v.tensor_scalar(out=sA[:], in0=sA[:], scalar1=-1.0, scalar2=None, op0=ALU.mult)
            c.wge(v, "q_ld", ld_consts)
            v.tensor_copy(out=sWdtb[:], in_=sWdtS[:])
            v.tensor_copy(out=sWxb[:], in_=sWxS[:])
            v.memset(sEps[:], 1e-5)
            c.inc(v.memset(sState[:], 0.0), "v_ms")
        es0w.close()

        if _STOP_PHASE < 1:
            _emit_taps()
            return nc
        with ExitStack() as es0:
            sXin = [es0.enter_context(nc.sbuf_tensor(f"sXin{i}", [P, D], F32)) for i in range(2)]
            psT = es0.enter_context(nc.psum_tensor("psT0", [P, D], F32))

            fence0b = c.snap()

            @block.sync
            def _(s):
                c.fence_wait(s, fence0b)
                for tt in range(8):
                    c.wge(s, "t_x", tt - 1)
                    c.dma(s, sXin[tt % 2][:], dr["x_t"][tt * P:(tt + 1) * P, :], "l_x")

            @block.tensor
            def _(t):
                c.wge(t, "q_ld", ld_consts)
                for tt in range(8):
                    c.wge(t, "l_x", 16 * (tt + 1))
                    c.wge(t, "v_x", tt)
                    for ch in range(NCH_D):
                        mm = t.transpose(psT[:, ch * P:(ch + 1) * P],
                                         sXin[tt % 2][:, ch * P:(ch + 1) * P], sEyeF[:])
                    c.inc(mm, "t_x")

            @block.vector
            def _(v):
                for tt in range(8):
                    c.wge(v, "t_x", tt + 1)
                    dst = sR[:].rearrange("p (ch t) -> p ch t", ch=NCH_D)[:, :, tt * P:(tt + 1) * P]
                    src = psT[:].rearrange("p (ch t) -> p ch t", ch=NCH_D)
                    c.inc(v.tensor_copy(out=dst, in_=src), "v_x")
                c.inc(v.engine_nop(), "v_ms")
            c.expect("t_x", 8)
            c.expect("v_x", 8)

        if _STOP_PHASE < 2:
            _emit_taps()
            return nc

        def emit_ln(tag, sXn, fence):
            with ExitStack() as esl:
                sSq = esl.enter_context(nc.sbuf_tensor(f"sSq{tag}", [P, TPC], F32))
                sStat = esl.enter_context(nc.sbuf_tensor(f"sStat{tag}", [1, 4 * TPC], F32))
                sStatB = esl.enter_context(nc.sbuf_tensor(f"sStatB{tag}", [1, 2 * TPC], BF16))
                psSum = esl.enter_context(nc.psum_tensor(f"psSum{tag}", [1, TPC], F32))
                psSq = esl.enter_context(nc.psum_tensor(f"psSq{tag}", [1, TPC], F32))
                psBrd = esl.enter_context(nc.psum_tensor(f"psBrd{tag}", [P, 2 * TPC], F32))
                b_tq = c.cnt("t_lnq")
                b_ts = c.cnt("t_lns")
                b_ss = c.cnt("s_lnq")
                b_vs = c.cnt("v_lns")
                b_sd = c.cnt("s_lnd")
                b_tb = c.cnt("t_lnb")
                b_vl = c.cnt("v_ln")

                MU = sStat[0:1, 0:TPC]
                MSQ = sStat[0:1, TPC:2 * TPC]
                MU2 = sStat[0:1, 2 * TPC:3 * TPC]
                VAR = sStat[0:1, TPC:2 * TPC]
                SD = sStat[0:1, TPC:2 * TPC]
                RSTD = sStat[0:1, TPC:2 * TPC]
                BNEG = sStat[0:1, 3 * TPC:4 * TPC]

                @block.scalar
                def _(sc):
                    c.fence_wait(sc, fence)
                    for ch in range(NCH_D):
                        c.wge(sc, "t_lnq", b_tq + ch)
                        c.inc(sc.activation(out=sSq[:], in_=sR[:, ch * TPC:(ch + 1) * TPC],
                                            func=AF.Square), "s_lnq")

                @block.tensor
                def _(t):
                    c.fence_wait(t, fence)
                    for ch in range(NCH_D):
                        for hh in range(2):
                            mm = t.matmul(psSum[:, hh * 512:(hh + 1) * 512], sOnesC[:],
                                          sR[:, ch * TPC + hh * 512: ch * TPC + (hh + 1) * 512],
                                          start=(ch == 0), stop=(ch == NCH_D - 1))
                    c.inc(mm, "t_lns")
                    for ch in range(NCH_D):
                        c.wge(t, "s_lnq", b_ss + ch + 1)
                        for hh in range(2):
                            mm = t.matmul(psSq[:, hh * 512:(hh + 1) * 512], sOnesC[:],
                                          sSq[:, hh * 512:(hh + 1) * 512],
                                          start=(ch == 0), stop=(ch == NCH_D - 1))
                        c.inc(mm, "t_lnq")

                @block.vector
                def _(v):
                    c.fence_wait(v, fence)
                    c.wge(v, "t_lns", b_ts + 1)
                    c.wge(v, "t_lnq", b_tq + NCH_D)
                    v.tensor_scalar(out=MU, in0=psSum[:, :], scalar1=1.0 / D, scalar2=None, op0=ALU.mult)
                    v.tensor_scalar(out=MSQ, in0=psSq[:, :], scalar1=1.0 / D, scalar2=None, op0=ALU.mult)
                    v.tensor_tensor(out=MU2, in0=MU, in1=MU, op=ALU.mult)
                    c.inc(v.tensor_tensor(out=VAR, in0=MSQ, in1=MU2, op=ALU.subtract), "v_lns")

                @block.scalar
                def _(sc):
                    c.wge(sc, "v_lns", b_vs + 1)
                    c.inc(sc.activation(out=SD, in_=VAR, func=AF.Sqrt, bias=sEps[0:1, :]), "s_lnd")

                @block.vector
                def _(v):
                    c.wge(v, "s_lnd", b_sd + 1)
                    v.reciprocal(out=RSTD, in_=SD)
                    v.scalar_tensor_tensor(out=BNEG, in0=MU, scalar=-1.0, in1=RSTD,
                                           op0=ALU.mult, op1=ALU.mult)
                    v.tensor_copy(out=sStatB[0:1, 0:TPC], in_=RSTD)
                    c.inc(v.tensor_copy(out=sStatB[0:1, TPC:2 * TPC], in_=BNEG), "v_lns")

                @block.tensor
                def _(t):
                    c.wge(t, "v_lns", b_vs + 2)
                    for q in range(4):
                        mm = t.matmul(psBrd[:, q * 512:(q + 1) * 512], sOnesR[:],
                                      sStatB[0:1, q * 512:(q + 1) * 512], start=True, stop=True)
                    c.inc(mm, "t_lnb")

                @block.vector
                def _(v):
                    c.wge(v, "t_lnb", b_tb + 1)
                    for ch in range(NCH_D):
                        v.tensor_tensor(out=sSq[:], in0=sR[:, ch * TPC:(ch + 1) * TPC],
                                        in1=psBrd[:, 0:TPC], op=ALU.mult)
                        last = v.tensor_tensor(out=sXn[:, ch * TPC:(ch + 1) * TPC], in0=sSq[:],
                                               in1=psBrd[:, TPC:2 * TPC], op=ALU.add)
                    c.inc(last, "v_ln")
                    c.inc(v.engine_nop(), "v_ms")
                c.expect("v_ln", b_vl + 1)

        def emit_ffn_a(first, sXn, fence, fence_io):
            wsrc1 = dr["w1s"] if first else w1bf
            wsrc3 = dr["w3s"] if first else w3bf
            with ExitStack() as esf:
                tg = "a" if first else "b"
                sUGs = esf.enter_context(nc.sbuf_tensor("sUGs" + tg, [P, TPC], BF16))
                sHc = esf.enter_context(nc.sbuf_tensor("sHc" + tg, [P, TPC], BF16))
                sS1e = esf.enter_context(nc.sbuf_tensor("sS1e" + tg, [P, TPC], BF16))
                sWb4 = [esf.enter_context(nc.sbuf_tensor(f"sWb4{tg}{i}", [P, 2048], BF16))
                        for i in range(4)]
                if first:
                    sWf4 = [esf.enter_context(nc.sbuf_tensor(f"sWf4{i}", [P, 2048], F32))
                            for i in range(4)]
                    sPf = [esf.enter_context(nc.sbuf_tensor(f"sPf{i}", [P, 2048], F32))
                           for i in range(2)]
                    sPb = [esf.enter_context(nc.sbuf_tensor(f"sPb{i}", [P, 2048], BF16))
                           for i in range(2)]
                    sW2e = esf.enter_context(nc.sbuf_tensor("sW2e", [P, 2048], BF16))
                    psW2 = esf.enter_context(nc.psum_tensor("psW2", [P, 1024], F32))
                psA = esf.enter_context(nc.psum_tensor("psAa" if first else "psAb", [P, TPC], F32))
                psB = esf.enter_context(nc.psum_tensor("psBa" if first else "psBb", [P, TPC], F32))
                psC = esf.enter_context(nc.psum_tensor("psCa" if first else "psCb", [P, TPC], F32))
                b_lw = c.cnt("l_wf")
                b_gc = c.cnt("g_wc")
                b_ta = c.cnt("t_fa")
                b_ss = c.cnt("s_fsi")
                b_vm = c.cnt("v_fmu")
                b_t1 = c.cnt("t_fs1")
                b_se = c.cnt("s_fev")
                b_gs = c.cnt("g_fs1")
                b_gw = c.cnt("g_fwc")
                b_lwp = c.cnt("l_wp")
                b_lpw = c.cnt("l_pw")
                b_lpp = c.cnt("l_pp")
                b_gpc = c.cnt("g_pc")
                b_tw2 = c.cnt("t_w2")
                b_sw2 = c.cnt("s_w2")
                b_gw2 = c.cnt("g_w2")
                NU = 2 * NCH_H

                @block.sync
                def _(s):
                    c.fence_wait(s, fence_io)
                    for u in range(NU):
                        m, hc = u % 2, u // 2
                        if first:
                            c.wge(s, "g_wc", b_gc + u - 3)
                            c.dma(s, sWf4[u % 4][:, 0:1024],
                                  (wsrc1 if m == 0 else wsrc3)[hc, :, 0:1024], "l_wf")
                            if m == 1:
                                c.wge(s, "g_pc", b_gpc + hc - 1)
                                c.dma(s, sPf[hc % 2][:, 0:1024],
                                      dr["w2g"][hc * P:(hc + 1) * P, 0:1024], "l_pw")
                        else:
                            c.wge(s, "t_fa", b_ta + u - 3)
                            c.dma(s, sWb4[u % 4][:, 0:1024],
                                  (wsrc1 if m == 0 else wsrc3)[hc, :, 0:1024], "l_wf")

                def emit_spill(g, hc):
                    c.wge(g, "s_fev", b_se + hc + 1)
                    c.dma(g, s1sp[hc, :, :], sS1e[:], "g_fs1")

                def emit_prep_pool(g, hc):
                    c.wge(g, "g_pc", b_gpc + hc - 1)
                    c.dma(g, sPf[hc % 2][:, 1024:2048],
                          dr["w2g"][hc * P:(hc + 1) * P, 1024:2048], "l_pp")
                    j = hc - 1
                    if j >= 0:
                        c.wge(g, "l_pw", b_lpw + 16 * (j + 1))
                        c.wge(g, "l_pp", b_lpp + 16 * (j + 1))
                        c.wge(g, "t_w2", b_tw2 + max(0, 2 * j - 2))
                        c.inc(g.tensor_copy(out=sPb[j % 2][:], in_=sPf[j % 2][:]), "g_pc")
                    j = hc - 2
                    if j >= 0:
                        c.wge(g, "s_w2", b_sw2 + 2 * j + 2)
                        dst = w2pbf.ap().rearrange("(cc p) d -> p cc d", p=P)[j * 2:j * 2 + 2, :, :]
                        c.dma(g, dst, sW2e[:], "g_w2")

                def emit_prep_pool_tail(g):
                    j = NCH_H - 1
                    c.wge(g, "l_pw", b_lpw + 16 * (j + 1))
                    c.wge(g, "l_pp", b_lpp + 16 * (j + 1))
                    c.wge(g, "t_w2", b_tw2 + 2 * j - 2)
                    c.inc(g.tensor_copy(out=sPb[j % 2][:], in_=sPf[j % 2][:]), "g_pc")
                    for j in (NCH_H - 2, NCH_H - 1):
                        c.wge(g, "s_w2", b_sw2 + 2 * j + 2)
                        dst = w2pbf.ap().rearrange("(cc p) d -> p cc d", p=P)[j * 2:j * 2 + 2, :, :]
                        c.dma(g, dst, sW2e[:], "g_w2")

                if first:
                    @block.gpsimd
                    def _(g):
                        c.fence_wait(g, fence_io)
                        for u in range(NU):
                            m, hc = u % 2, u // 2
                            c.wge(g, "g_wc", b_gc + u - 3)
                            c.dma(g, sWf4[u % 4][:, 1024:2048],
                                  (wsrc1 if m == 0 else wsrc3)[hc, :, 1024:2048], "l_wp")
                            if m == 1:
                                emit_prep_pool(g, hc)
                                if hc >= 1:
                                    emit_spill(g, hc - 1)
                            c.wge(g, "l_wf", b_lw + 16 * (u + 1))
                            c.wge(g, "l_wp", b_lwp + 16 * (u + 1))
                            c.wge(g, "t_fa", b_ta + u - 3)
                            c.wge(g, "g_fwc", b_gw + 16 * (u - 1))
                            c.inc(g.tensor_copy(out=sWb4[u % 4][:], in_=sWf4[u % 4][:]), "g_wc")
                            c.dma(g, (w1bf if m == 0 else w3bf)[hc, :, :], sWb4[u % 4][:], "g_fwc")
                        emit_prep_pool_tail(g)
                        emit_spill(g, NCH_H - 1)
                        c.inc(g.engine_nop(), "g_ms")
                else:
                    @block.gpsimd
                    def _(g):
                        c.fence_wait(g, fence_io)
                        for u in range(NU):
                            m, hc = u % 2, u // 2
                            if m == 1 and hc >= 1:
                                emit_spill(g, hc - 1)
                            c.wge(g, "t_fa", b_ta + u - 3)
                            c.dma(g, sWb4[u % 4][:, 1024:2048],
                                  (wsrc1 if m == 0 else wsrc3)[hc, :, 1024:2048], "l_wp")
                        emit_spill(g, NCH_H - 1)
                        c.inc(g.engine_nop(), "g_ms")

                def emit_prep_mm(t, j):
                    c.wge(t, "g_pc", b_gpc + j + 1)
                    for h in range(2):
                        c.wge(t, "s_w2", b_sw2 + 2 * j + h)
                        for q in range(2):
                            mm = t.matmul(psW2[:, q * 512:(q + 1) * 512], sBD64[:],
                                          sPb[j % 2][:, h * 1024 + q * 512: h * 1024 + (q + 1) * 512],
                                          start=True, stop=True)
                        c.inc(mm, "t_w2")

                @block.tensor
                def _(t):
                    c.fence_wait(t, fence)
                    for hc in range(NCH_H):
                        for m in range(2):
                            u = 2 * hc + m
                            if first:
                                c.wge(t, "g_wc", b_gc + u + 1)
                            else:
                                c.wge(t, "l_wf", b_lw + 16 * (u + 1))
                                c.wge(t, "l_wp", b_lwp + 16 * (u + 1))
                            if m == 0:
                                c.wge(t, "s_fsi", b_ss + hc)
                            else:
                                c.wge(t, "v_fmu", b_vm + hc)
                            ps = psA if m == 0 else psB
                            for kc in range(NCH_D):
                                for q in range(2):
                                    mm = t.matmul(ps[:, q * 512:(q + 1) * 512],
                                                  sWb4[u % 4][:, kc * P:(kc + 1) * P],
                                                  sXn[:, kc * TPC + q * 512: kc * TPC + (q + 1) * 512],
                                                  start=(kc == 0), stop=(kc == NCH_D - 1))
                            c.inc(mm, "t_fa")
                        c.wge(t, "v_fmu", b_vm + hc + 1)
                        c.wge(t, "s_fev", b_se + hc)
                        for q in range(2):
                            mm = t.matmul(psC[:, q * 512:(q + 1) * 512], sH128[:],
                                          sHc[:, q * 512:(q + 1) * 512], start=True, stop=True)
                        c.inc(mm, "t_fs1")
                        if first and hc >= 1:
                            emit_prep_mm(t, hc - 1)
                    if first:
                        emit_prep_mm(t, NCH_H - 1)

                @block.scalar
                def _(sc):
                    c.fence_wait(sc, fence)
                    for hc in range(NCH_H):
                        c.wge(sc, "t_fa", b_ta + 2 * hc + 1)
                        c.wge(sc, "v_fmu", b_vm + hc)
                        c.inc(sc.activation(out=sUGs[:], in_=psA[:, :], func=AF.Silu), "s_fsi")
                        c.wge(sc, "t_fs1", b_t1 + hc + 1)
                        c.wge(sc, "g_fs1", b_gs + 16 * hc)
                        c.inc(sc.activation(out=sS1e[:], in_=psC[:, :], func=AF.Copy,
                                            scale=INV_SQRT_N), "s_fev")
                        if first and hc >= 1:
                            j = hc - 1
                            for h in range(2):
                                c.wge(sc, "t_w2", b_tw2 + 2 * j + h + 1)
                                if h == 0:
                                    c.wge(sc, "g_w2", b_gw2 + 16 * j)
                                c.inc(sc.activation(out=sW2e[:, h * 1024:(h + 1) * 1024],
                                                    in_=psW2[:, :], func=AF.Copy), "s_w2")
                    if first:
                        j = NCH_H - 1
                        for h in range(2):
                            c.wge(sc, "t_w2", b_tw2 + 2 * j + h + 1)
                            if h == 0:
                                c.wge(sc, "g_w2", b_gw2 + 16 * j)
                            c.inc(sc.activation(out=sW2e[:, h * 1024:(h + 1) * 1024],
                                                in_=psW2[:, :], func=AF.Copy), "s_w2")

                @block.vector
                def _(v):
                    c.fence_wait(v, fence)
                    for hc in range(NCH_H):
                        c.wge(v, "s_fsi", b_ss + hc + 1)
                        c.wge(v, "t_fa", b_ta + 2 * hc + 2)
                        c.wge(v, "t_fs1", b_t1 + hc)
                        c.inc(v.tensor_tensor(out=sHc[:], in0=sUGs[:], in1=psB[:, :], op=ALU.mult),
                              "v_fmu")
                    c.inc(v.engine_nop(), "v_ms")

                c.expect("t_fa", b_ta + NU)
                c.expect("t_fs1", b_t1 + NCH_H)
                c.expect("s_fsi", b_ss + NCH_H)
                c.expect("s_fev", b_se + NCH_H)
                c.expect("v_fmu", b_vm + NCH_H)
                c.expect("g_fs1", b_gs + 16 * NCH_H)
                if first:
                    c.expect("t_w2", b_tw2 + 2 * NCH_H)
                    c.expect("s_w2", b_sw2 + 2 * NCH_H)
                    c.expect("g_w2", b_gw2 + 16 * NCH_H)
                    c.expect("g_pc", b_gpc + NCH_H)
            return b_gs

        def emit_ffn_b(first, b_gs):
            with ExitStack() as esb:
                sS1tc = esb.enter_context(
                    nc.sbuf_tensor("sS1t" + ("a" if first else "b"), [P, NCH_H * 512], BF16))
                psO = [esb.enter_context(
                    nc.psum_tensor(("psOa" if first else "psOb") + str(j), [P, 512], F32))
                    for j in range(4)]
                b_l1 = c.cnt("l_s1")
                b_lw2 = c.cnt("l_w2")
                b_tb = c.cnt("t_fb")
                b_vb = c.cnt("v_fb")
                se_a_done = c.cnt("s_fev")
                vm_a_done = c.cnt("v_fmu")

                fence_b = c.snap()

                b_l1p = c.cnt("l_s1p")
                b_lw2p = c.cnt("l_w2p")
                sBsl = [esb.enter_context(nc.sbuf_tensor(f"sBsl{first}{i}", [P, 512], BF16))
                        for i in range(4)]

                def wsl(i, a=0, b=512):
                    return sBsl[i % 4][:, a:b]

                @block.sync
                def _(s):
                    c.fence_wait(s, fence_b)
                    for tc in range(2):
                        if tc == 1:
                            c.wge(s, "t_fb", b_tb + 256)
                        for hc in range(0, NCH_H, 2):
                            c.wge(s, "g_fs1", b_gs + 16 * (hc + 1))
                            c.dma(s, sS1tc[:, hc * 512:(hc + 1) * 512],
                                  s1sp[hc, :, tc * 512:(tc + 1) * 512], "l_s1")
                        for k in range(256):
                            dg, cb = k // NCH_H, k % NCH_H
                            gk = tc * 256 + k
                            if gk % 2 == 0:
                                c.wge(s, "t_fb", b_tb + gk - 3)
                                c.dma(s, wsl(gk),
                                      w2pbf[cb * P:(cb + 1) * P, dg * 512:(dg + 1) * 512], "l_w2")

                @block.gpsimd
                def _(g):
                    c.fence_wait(g, fence_b)
                    for tc in range(2):
                        if tc == 1:
                            c.wge(g, "t_fb", b_tb + 256)
                        for hc in range(1, NCH_H, 2):
                            c.wge(g, "g_fs1", b_gs + 16 * (hc + 1))
                            c.dma(g, sS1tc[:, hc * 512:(hc + 1) * 512],
                                  s1sp[hc, :, tc * 512:(tc + 1) * 512], "l_s1p")
                        for k in range(256):
                            dg, cb = k // NCH_H, k % NCH_H
                            gk = tc * 256 + k
                            if gk % 2 == 1:
                                c.wge(g, "t_fb", b_tb + gk - 3)
                                c.dma(g, wsl(gk),
                                      w2pbf[cb * P:(cb + 1) * P, dg * 512:(dg + 1) * 512], "l_w2p")
                    c.inc(g.engine_nop(), "g_ms")

                @block.tensor
                def _(t):
                    c.wge(t, "s_fev", se_a_done)
                    c.wge(t, "v_fmu", vm_a_done)
                    for tc in range(2):
                        c.wge(t, "l_s1", b_l1 + 16 * (NCH_H // 2) * (tc + 1))
                        c.wge(t, "l_s1p", b_l1p + 16 * (NCH_H // 2) * (tc + 1))
                        for k in range(256):
                            dg, cb = k // NCH_H, k % NCH_H
                            gk = tc * 256 + k
                            if gk % 2 == 0:
                                c.wge(t, "l_w2", b_lw2 + 16 * (gk // 2 + 1))
                            else:
                                c.wge(t, "l_w2p", b_lw2p + 16 * (gk // 2 + 1))
                            if cb == 0:
                                c.wge(t, "v_fb", b_vb + tc * 4 + dg)
                            for j in range(4):
                                mm = t.matmul(psO[j][:, :], wsl(gk, j * P, (j + 1) * P),
                                              sS1tc[:, cb * 512:(cb + 1) * 512],
                                              start=(cb == 0), stop=(cb == NCH_H - 1))
                            c.inc(mm, "t_fb")

                @block.vector
                def _(v):
                    for tc in range(2):
                        for dg in range(4):
                            c.wge(v, "t_fb", b_tb + tc * 256 + (dg + 1) * NCH_H)
                            for j in range(4):
                                dc = dg * 4 + j
                                sl = sR[:, dc * TPC + tc * 512: dc * TPC + (tc + 1) * 512]
                                last = v.scalar_tensor_tensor(out=sl, in0=psO[j][:, :], scalar=0.5,
                                                              in1=sl, op0=ALU.mult, op1=ALU.add)
                            c.inc(last, "v_fb")
                    c.inc(v.engine_nop(), "v_ms")
                c.expect("t_fb", b_tb + 512)
                c.expect("v_fb", b_vb + 8)

        if _STOP_PHASE < 3:
            _emit_taps()
            return nc
        fence = c.snap()
        with ExitStack() as esx:
            sXn = esx.enter_context(nc.sbuf_tensor("sXn1", [P, NCH_D * TPC], BF16))
            emit_ln("1", sXn, fence)
            fence = c.snap()
            ffn1_bgs = emit_ffn_a(True, sXn, fence, fence)
        if _STOP_PHASE < 4:
            _emit_taps()
            return nc
        emit_ffn_b(True, ffn1_bgs)

        if _STOP_PHASE < 5:
            _emit_taps()
            return nc
        fence = c.snap()
        with ExitStack() as esx2:
            sXn2 = esx2.enter_context(nc.sbuf_tensor("sXn2", [P, NCH_D * TPC], BF16))
            emit_ln("2", sXn2, fence)
            v_ln2 = c.cnt("v_ln")

            @block.gpsimd
            def _(g):
                c.wge(g, "v_ln", v_ln2)
                for ch in range(NCH_D):
                    c.dma(g, ag_in[ch * P:(ch + 1) * P, :], sXn2[:, ch * TPC:(ch + 1) * TPC], "g_ag")
                c.wge(g, "g_ag", c.cnt("g_ag"))
                cc_i = g.collective_compute(
                    "AllGather", ALU.bypass,
                    replica_groups=[list(range(NCORES))],
                    ins=[ag_in.ap().opt()], outs=[ag_out.ap().opt()],
                )
                cc_i.then_inc(c.sem("cag"))
                c.n["cag"] += 1
                c.inc(g.engine_nop(), "g_ms")

        if _STOP_PHASE < 6:
            _emit_taps()
            return nc
        with ExitStack() as esm:
            if True:
                sWf = [esm.enter_context(nc.sbuf_tensor(f"sWfF{i}", [P, 2048], F32)) for i in range(2)]
                sWb = [esm.enter_context(nc.sbuf_tensor(f"sWbF{i}", [P, 2048], BF16)) for i in range(2)]
                sRhs = esm.enter_context(nc.sbuf_tensor("sRhs", [P, NCH_D * TPC], BF16))
                sU4 = esm.enter_context(nc.sbuf_tensor("sU4", [P, NDT * TPC], BF16))
                sHt = esm.enter_context(nc.sbuf_tensor("sHt", [P, TPC + 4], F32))
                sCac = esm.enter_context(nc.sbuf_tensor("sCac", [P, TPC], F32))
                sTails = esm.enter_context(nc.sbuf_tensor("sTails", [P, NDT * 4], F32))
                sGe = esm.enter_context(nc.sbuf_tensor("sGe", [P, TPC], BF16))
                sXe = esm.enter_context(nc.sbuf_tensor("sXe", [P, TPC], F32))
                sXe2 = esm.enter_context(nc.sbuf_tensor("sXe2", [32, TPC], F32))
                psP = [esm.enter_context(nc.psum_tensor(f"psP{i}", [P, TPC], F32)) for i in range(2)]
                psX = esm.enter_context(nc.psum_tensor("psX", [P, TPC], F32))
                psX2 = esm.enter_context(nc.psum_tensor("psX2", [32, TPC], F32))
                b_lr = c.cnt("l_rhs")
                b_lw = c.cnt("l_wf")
                b_gc = c.cnt("g_wc")
                b_tp = c.cnt("t_pj")
                b_tx = c.cnt("t_wx")
                b_vc = c.cnt("v_cv")
                b_su = c.cnt("s_u")
                b_sg = c.cnt("s_g")
                b_vx = c.cnt("v_xe")
                b_gu = c.cnt("g_u")
                b_gg = c.cnt("g_g")
                b_gx = c.cnt("g_x")
                vfb_done = c.cnt("v_fb")
                if _DEBUG_TAPS:
                    c.dbg_sb += [("sXe", sXe, [P, TPC], F32), ("sXe2", sXe2, [32, TPC], F32),
                                 ("sU4", sU4, [P, NDT * TPC], BF16), ("sWxb", sWxb, [P, NDT * 160], BF16),
                                 ("sTails", sTails, [P, NDT * 4], F32), ("sHt", sHt, [P, TPC + 4], F32)]

                @block.vector
                def _(v):
                    c.inc(v.memset(sTails[:], 0.0), "v_tl")

                fence_fr = c.snap()

                b_lrs = c.cnt("l_rhs2")
                b_lwp = c.cnt("l_wp")

                @block.sync
                def _(s):
                    c.fence_wait(s, fence_fr)
                    s.wait_ge(c.sem("cag"), 1)
                    for cb in range(NCORES):
                        if cb >= 1:
                            c.wge(s, "t_pj", b_tp + 8 * (cb - 1) + 8)
                        for kc in range(0, NCH_D, 2):
                            c.dma(s, sRhs[:, kc * TPC:(kc + 1) * TPC],
                                  ag_out[cb * D + kc * P: cb * D + (kc + 1) * P, :], "l_rhs")
                        for oc in range(8):
                            u = cb * 8 + oc
                            c.wge(s, "g_wc", b_gc + u - 1)
                            c.dma(s, sWf[u % 2][:, 0:1024], dr["wins"][oc, :, 0:1024], "l_wf")

                def emit_front_dmas(g, cb):
                    for dt in range(NDT):
                        idx = cb * 4 + dt
                        c.wge(g, "s_u", b_su + idx + 1)
                        c.dma(g, usp[dt * P:(dt + 1) * P, cb * TPC:(cb + 1) * TPC],
                              sU4[:, dt * TPC:(dt + 1) * TPC], "g_u")
                    for oc in range(4, 8):
                        gidx = cb * 4 + (oc - 4)
                        dtg = oc - 4
                        c.wge(g, "s_g", b_sg + gidx + 1)
                        c.dma(g, gsp[dtg * P:(dtg + 1) * P, cb * TPC:(cb + 1) * TPC], sGe[:], "g_g")
                    c.wge(g, "v_xe", b_vx + 2 * (cb + 1))
                    c.dma(g, ar_in[0:P, cb * TPC:(cb + 1) * TPC], sXe[:], "g_x")
                    c.dma(g, ar_in[P:160, cb * TPC:(cb + 1) * TPC], sXe2[:], "g_x")

                @block.gpsimd
                def _(g):
                    for cb in range(NCORES):
                        if cb >= 1:
                            emit_front_dmas(g, cb - 1)
                        for oc in range(8):
                            u = cb * 8 + oc
                            c.wge(g, "g_wc", b_gc + u - 1)
                            c.dma(g, sWf[u % 2][:, 1024:2048], dr["wins"][oc, :, 1024:2048], "l_wp")
                            c.wge(g, "l_wf", b_lw + 16 * (u + 1))
                            c.wge(g, "l_wp", b_lwp + 16 * (u + 1))
                            c.wge(g, "t_pj", b_tp + u - 1)
                            c.inc(g.tensor_copy(out=sWb[u % 2][:], in_=sWf[u % 2][:]), "g_wc")
                    emit_front_dmas(g, NCORES - 1)
                    c.wge(g, "g_x", b_gx + 16 * 16)
                    cc_i = g.collective_compute(
                        "AllReduce", ALU.add,
                        replica_groups=[list(range(NCORES))],
                        ins=[ar_in.ap().opt()], outs=[ar_out.ap().opt()],
                    )
                    cc_i.then_inc(c.sem("car"))
                    c.n["car"] += 1
                    c.inc(g.engine_nop(), "g_ms")

                @block.tensor
                def _(t):
                    c.wge(t, "v_fb", vfb_done)
                    c.wge(t, "v_tl", 1)
                    for cb in range(NCORES):
                        c.wge(t, "l_rhs", b_lr + 16 * (NCH_D // 2) * (cb + 1))
                        c.wge(t, "l_rhs2", b_lrs + 16 * (NCH_D // 2) * (cb + 1))
                        for oc in range(8):
                            u = cb * 8 + oc
                            c.wge(t, "g_wc", b_gc + u + 1)
                            up = u - 2
                            if up >= 0:
                                ocp, cbp = up % 8, up // 8
                                if ocp < 4:
                                    c.wge(t, "v_cv", b_vc + cbp * 4 + ocp + 1)
                                else:
                                    c.wge(t, "s_g", b_sg + cbp * 4 + (ocp - 4) + 1)
                            for kc in range(NCH_D):
                                for q in range(2):
                                    mm = t.matmul(psP[u % 2][:, q * 512:(q + 1) * 512],
                                                  sWb[u % 2][:, kc * P:(kc + 1) * P],
                                                  sRhs[:, kc * TPC + q * 512: kc * TPC + (q + 1) * 512],
                                                  start=(kc == 0), stop=(kc == NCH_D - 1))
                            c.inc(mm, "t_pj")
                        c.wge(t, "s_u", b_su + cb * 4 + 4)
                        c.wge(t, "v_xe", b_vx + 2 * cb)
                        for j in range(2):
                            w = P if j == 0 else 32
                            ps = psX if j == 0 else psX2
                            for dt in range(NDT):
                                for q in range(2):
                                    mm = t.matmul(ps[:, q * 512:(q + 1) * 512],
                                                  sWxb[:, dt * 160 + j * P: dt * 160 + j * P + w],
                                                  sU4[:, dt * TPC + q * 512: dt * TPC + (q + 1) * 512],
                                                  start=(dt == 0), stop=(dt == NDT - 1))
                        c.inc(mm, "t_wx")

                @block.vector
                def _(v):
                    for cb in range(NCORES):
                        for dt in range(NDT):
                            u = cb * 8 + dt
                            idx = cb * 4 + dt
                            c.wge(v, "t_pj", b_tp + u + 1)
                            c.wge(v, "s_u", b_su + idx)
                            v.tensor_copy(out=sHt[:, 0:3], in_=sTails[:, dt * 4:dt * 4 + 3])
                            v.tensor_copy(out=sTails[:, dt * 4:dt * 4 + 3],
                                          in_=psP[u % 2][:, TPC - 3:TPC])
                            v.tensor_copy(out=sHt[:, 3:3 + TPC], in_=psP[u % 2][:, :])
                            v.tensor_scalar(out=sCac[:], in0=sHt[:, 0:TPC],
                                            scalar1=sCw[:, dt * 4:dt * 4 + 1], scalar2=None, op0=ALU.mult)
                            for k in range(1, 4):
                                last = v.scalar_tensor_tensor(out=sCac[:], in0=sHt[:, k:k + TPC],
                                                              scalar=sCw[:, dt * 4 + k:dt * 4 + k + 1],
                                                              in1=sCac[:], op0=ALU.mult, op1=ALU.add)
                            c.inc(last, "v_cv")
                        c.wge(v, "t_wx", b_tx + cb + 1)
                        c.wge(v, "g_x", b_gx + 16 * 2 * cb)
                        v.tensor_copy(out=sXe[:], in_=psX[:, :])
                        c.inc(v.tensor_copy(out=sXe2[:], in_=psX2[:, :]), "v_xe", 2)
                    c.inc(v.engine_nop(), "v_ms")

                @block.scalar
                def _(sc):
                    sc.wait_ge(c.sem("cag"), 1)
                    for cb in range(NCORES):
                        if cb >= 1:
                            c.wge(sc, "t_pj", b_tp + 8 * (cb - 1) + 8)
                        for kc in range(1, NCH_D, 2):
                            c.dma(sc, sRhs[:, kc * TPC:(kc + 1) * TPC],
                                  ag_out[cb * D + kc * P: cb * D + (kc + 1) * P, :], "l_rhs2")
                        for dt in range(NDT):
                            idx = cb * 4 + dt
                            c.wge(sc, "v_cv", b_vc + idx + 1)
                            c.wge(sc, "t_wx", b_tx + cb)
                            if cb >= 1:
                                c.wge(sc, "g_u", b_gu + 16 * (4 * (cb - 1) + dt + 1))
                            c.inc(sc.activation(out=sU4[:, dt * TPC:(dt + 1) * TPC], in_=sCac[:],
                                                func=AF.Silu, bias=sCb[:, dt:dt + 1]), "s_u")
                        for oc in range(4, 8):
                            u = cb * 8 + oc
                            gidx = cb * 4 + (oc - 4)
                            c.wge(sc, "t_pj", b_tp + u + 1)
                            c.wge(sc, "g_g", b_gg + 16 * gidx)
                            c.inc(sc.activation(out=sGe[:], in_=psP[u % 2][:, :], func=AF.Copy), "s_g")


                c.expect("t_pj", b_tp + 64)
                c.expect("v_cv", b_vc + 32)
                c.expect("s_u", b_su + 32)
                c.expect("s_g", b_sg + 32)
                c.expect("v_xe", b_vx + 16)
                c.expect("g_u", b_gu + 16 * 32)
                c.expect("g_g", b_gg + 16 * 32)
                c.expect("g_x", b_gx + 16 * 16)

        if _STOP_PHASE < 7:
            _emit_taps()
            return nc
        sTs = E(nc.sbuf_tensor("sTs", [P, L], BF16))
        with ExitStack() as esmid:
            sStg = esmid.enter_context(nc.sbuf_tensor("sStg", [P, 4096], F32))
            b_lt = c.cnt("l_ts")
            b_vt = c.cnt("v_ts")

            fence_mid = c.snap()

            @block.sync
            def _(s):
                c.fence_wait(s, fence_mid)
                s.wait_ge(c.sem("car"), 1)
                for i in range(2):
                    c.wge(s, "v_ts", b_vt + i)
                    c.dma(s, sStg[:], ar_out[0:P, i * 4096:(i + 1) * 4096], "l_ts")

            @block.vector
            def _(v):
                for i in range(2):
                    c.wge(v, "l_ts", b_lt + 16 * (i + 1))
                    c.inc(v.tensor_copy(out=sTs[:, i * 4096:(i + 1) * 4096], in_=sStg[:]), "v_ts")
                c.inc(v.engine_nop(), "v_ms")
            c.expect("v_ts", b_vt + 2)

        if _STOP_PHASE < 8:
            _emit_taps()
            return nc
        fence = c.snap()
        with ExitStack() as ess:
            sBCs = ess.enter_context(nc.sbuf_tensor("sBCs", [32, TSC], F32))
            sBCt = ess.enter_context(nc.sbuf_tensor("sBCt", [32, TSC], BF16))
            sBbc = ess.enter_context(nc.sbuf_tensor("sBbc", [P, NS * TSC], BF16))
            sCbc = ess.enter_context(nc.sbuf_tensor("sCbc", [P, NS * TSC], BF16))
            sDtSl = ess.enter_context(nc.sbuf_tensor("sDtSl", [P, TSC], F32))
            sT1 = ess.enter_context(nc.sbuf_tensor("sT1", [P, TSC], F32))
            sT2 = ess.enter_context(nc.sbuf_tensor("sT2", [P, TSC], F32))
            sDtu = ess.enter_context(nc.sbuf_tensor("sDtu", [P, TSC], BF16))
            sdBu = ess.enter_context(nc.sbuf_tensor("sdBu", [P, TSC], BF16))
            sdA = [ess.enter_context(nc.sbuf_tensor(f"sdA{i}", [P, TSC], F32)) for i in range(2)]
            sSoA = ess.enter_context(nc.sbuf_tensor("sSoA", [P, NS * TSC], BF16))
            sSC = ess.enter_context(nc.sbuf_tensor("sSC", [P, TSC], BF16))
            sUt = [ess.enter_context(nc.sbuf_tensor(f"sUt{i}", [P, TSC], BF16)) for i in range(2)]
            sGt = [ess.enter_context(nc.sbuf_tensor(f"sGt{i}", [P, TSC], BF16)) for i in range(2)]
            sSg = ess.enter_context(nc.sbuf_tensor("sSg", [P, TSC], F32))
            sY1 = ess.enter_context(nc.sbuf_tensor("sY1", [P, TSC], F32))
            sYp = ess.enter_context(nc.sbuf_tensor("sYp", [P, TSC], BF16))
            psBc = [ess.enter_context(nc.psum_tensor(f"psBc{i}", [P, TSC], F32)) for i in range(2)]
            psDt = ess.enter_context(nc.psum_tensor("psDt", [P, TSC], F32))
            psY = [ess.enter_context(nc.psum_tensor(f"psYs{i}", [P, TSC], F32)) for i in range(2)]
            b_lu = c.cnt("l_ug")
            b_lbc = c.cnt("l_bc")
            b_vcc = c.cnt("v_bcc")
            b_tb = c.cnt("t_bc")
            b_vb = c.cnt("v_bc")
            b_td = c.cnt("t_dt")
            b_sp = c.cnt("s_sp")
            b_vd = c.cnt("v_dts")
            b_sa = c.cnt("s_dA")
            b_vn = c.cnt("v_scn")
            b_vq = c.cnt("v_sc")
            b_tr = c.cnt("t_rd")
            b_vs = c.cnt("v_stt")
            b_vy = c.cnt("v_yp")
            b_ssg = c.cnt("s_sg")
            b_gy = c.cnt("g_yp")

            b_lgs = c.cnt("l_gs")

            @block.sync
            def _(s):
                c.fence_wait(s, fence)
                for k in range(NTCH * NDT):
                    tch, dt = k // NDT, k % NDT
                    if dt == 0:
                        c.wge(s, "v_bcc", b_vcc + tch)
                        c.dma(s, sBCs[:], ar_out[P:160, tch * TSC:(tch + 1) * TSC], "l_bc")
                    c.wge(s, "v_stt", b_vs + k - 1)
                    c.dma(s, sUt[k % 2][:], usp[dt * P:(dt + 1) * P, tch * TSC:(tch + 1) * TSC], "l_ug")

            @block.tensor
            def _(t):
                c.fence_wait(t, fence)
                for tch in range(NTCH):
                    c.wge(t, "v_bcc", b_vcc + tch + 1)
                    for n in range(2 * NS):
                        bn = tch * 2 * NS + n
                        c.wge(t, "v_bc", b_vb + bn - 1)
                        mm = t.matmul(psBc[bn % 2][:, :], sEsel[:, n * P:(n + 1) * P],
                                      sBCt[:, :], start=True, stop=True)
                        c.inc(mm, "t_bc")
                    for dt in range(NDT):
                        k = tch * NDT + dt
                        c.wge(t, "s_sp", b_sp + k)
                        mm = t.matmul(psDt[:, :], sWdtb[:, dt * P:(dt + 1) * P],
                                      sTs[:, tch * TSC:(tch + 1) * TSC], start=True, stop=True)
                        c.inc(mm, "t_dt")
                        c.wge(t, "v_stt", b_vs + k - 1)
                        for n in range(NS):
                            c.wge(t, "v_sc", b_vq + k * NS + n + 1)
                            mm = t.matmul(psY[k % 2][:, :], sEyeB[:], sSC[:, :],
                                          start=(n == 0), stop=(n == NS - 1))
                            c.inc(mm, "t_rd")

            @block.scalar
            def _(sc):
                c.fence_wait(sc, fence)
                for tch in range(NTCH):
                    for dt in range(NDT):
                        k = tch * NDT + dt
                        c.wge(sc, "v_yp", b_vy + k - 1)
                        c.dma(sc, sGt[k % 2][:], gsp[dt * P:(dt + 1) * P, tch * TSC:(tch + 1) * TSC], "l_gs")
                        c.wge(sc, "t_dt", b_td + k + 1)
                        c.wge(sc, "v_dts", b_vd + k)
                        sc.activation(out=sT1[:], in_=psDt[:, :], func=AF.Abs, bias=sBdt[:, dt:dt + 1])
                        sc.activation(out=sT2[:], in_=sT1[:], func=AF.Exp, scale=-1.0)
                        sc.activation(out=sT2[:], in_=sT2[:], func=AF.Ln, bias=1.0)
                        c.inc(sc.activation(out=sT1[:], in_=psDt[:, :], func=AF.Relu,
                                            bias=sBdt[:, dt:dt + 1]), "s_sp")
                        c.wge(sc, "v_dts", b_vd + k + 1)
                        for n in range(NS):
                            if n >= 2:
                                c.wge(sc, "v_scn", b_vn + k * NS + n - 1)
                            c.inc(sc.activation(out=sdA[n % 2][:], in_=sDtSl[:], func=AF.Exp,
                                                scale=sA[:, dt * NS + n: dt * NS + n + 1]), "s_dA")
                        c.wge(sc, "v_yp", b_vy + k)
                        c.inc(sc.activation(out=sSg[:], in_=sGt[k % 2][:], func=AF.Silu), "s_sg")

            @block.vector
            def _(v):
                c.fence_wait(v, fence)
                for tch in range(NTCH):
                    c.wge(v, "l_bc", b_lbc + 16 * (tch + 1))
                    c.wge(v, "t_bc", b_tb + 32 * tch)
                    c.inc(v.tensor_copy(out=sBCt[:], in_=sBCs[:]), "v_bcc")
                    for n in range(2 * NS):
                        bn = tch * 2 * NS + n
                        c.wge(v, "t_bc", b_tb + bn + 1)
                        dst = sBbc if n < NS else sCbc
                        nn = n % NS
                        c.inc(v.tensor_copy(out=dst[:, nn * TSC:(nn + 1) * TSC],
                                            in_=psBc[bn % 2][:, :]), "v_bc")
                    for dt in range(NDT):
                        k = tch * NDT + dt
                        c.wge(v, "s_sp", b_sp + k + 1)
                        c.wge(v, "l_ug", b_lu + 16 * (k + 1))
                        v.tensor_add(out=sDtSl[:], in0=sT1[:], in1=sT2[:])
                        c.inc(v.tensor_tensor(out=sDtu[:], in0=sDtSl[:], in1=sUt[k % 2][:],
                                              op=ALU.mult), "v_dts")
                        for n in range(NS):
                            v.tensor_tensor(out=sdBu[:], in0=sDtu[:],
                                            in1=sBbc[:, n * TSC:(n + 1) * TSC], op=ALU.mult)
                            c.wge(v, "s_dA", b_sa + k * NS + n + 1)
                            c.inc(v.tensor_tensor_scan(
                                out=sSoA[:, n * TSC:(n + 1) * TSC], data0=sdA[n % 2][:],
                                data1=sdBu[:], initial=sState[:, dt * NS + n: dt * NS + n + 1],
                                op0=ALU.mult, op1=ALU.add), "v_scn")
                            c.wge(v, "t_rd", b_tr + k * NS + n)
                            c.inc(v.tensor_tensor(out=sSC[:], in0=sSoA[:, n * TSC:(n + 1) * TSC],
                                                  in1=sCbc[:, n * TSC:(n + 1) * TSC], op=ALU.mult),
                                  "v_sc")
                        c.wge(v, "t_rd", b_tr + (k + 1) * NS)
                        c.inc(v.scalar_tensor_tensor(out=sY1[:], in0=sUt[k % 2][:],
                                                     scalar=sDp[:, dt:dt + 1], in1=psY[k % 2][:, :],
                                                     op0=ALU.mult, op1=ALU.add), "v_stt")
                        src = sSoA[:].rearrange("p (n t) -> p n t", n=NS)[:, :, TSC - 1:TSC]
                        v.tensor_copy(out=sState[:, dt * NS:(dt + 1) * NS], in_=src)
                        c.wge(v, "s_sg", b_ssg + k + 1)
                        c.wge(v, "g_yp", b_gy + 16 * k)
                        c.inc(v.tensor_tensor(out=sYp[:], in0=sY1[:], in1=sSg[:], op=ALU.mult), "v_yp")
                c.inc(v.engine_nop(), "v_ms")

            @block.gpsimd
            def _(g):
                for k in range(NTCH * NDT):
                    tch, dt = k // NDT, k % NDT
                    c.wge(g, "v_yp", b_vy + k + 1)
                    dst_core = tch // 2
                    coff = (tch % 2) * TSC
                    c.dma(g, a2a_in[dst_core * DSC + dt * P: dst_core * DSC + (dt + 1) * P,
                                    coff:coff + TSC], sYp[:], "g_yp")
                c.wge(g, "g_yp", b_gy + 16 * NTCH * NDT)
                with allow_ici_all_to_all():
                    cc_i = g.collective_compute(
                        "AllToAll", ALU.bypass,
                        replica_groups=[list(range(NCORES))],
                        ins=[a2a_in.ap().opt()], outs=[a2a_out.ap().opt()],
                    )
                cc_i.then_inc(c.sem("ca2a"))
                c.n["ca2a"] += 1
                c.inc(g.engine_nop(), "g_ms")
            c.expect("t_bc", b_tb + NTCH * 2 * NS)
            c.expect("v_bc", b_vb + NTCH * 2 * NS)
            c.expect("t_rd", b_tr + NTCH * NDT * NS)
            c.expect("v_scn", b_vn + NTCH * NDT * NS)
            c.expect("v_sc", b_vq + NTCH * NDT * NS)
            c.expect("v_stt", b_vs + NTCH * NDT)
            c.expect("v_yp", b_vy + NTCH * NDT)
            c.expect("s_dA", b_sa + NTCH * NDT * NS)
            c.expect("s_sg", b_ssg + NTCH * NDT)
            c.expect("g_yp", b_gy + 16 * NTCH * NDT)

        if _STOP_PHASE < 9:
            _emit_taps()
            return nc
        fence = c.snap()
        with ExitStack() as eso:
            sWf = [eso.enter_context(nc.sbuf_tensor(f"sWfO{i}", [P, 512], F32)) for i in range(2)]
            sWb = [eso.enter_context(nc.sbuf_tensor(f"sWbO{i}", [P, 512], BF16)) for i in range(2)]
            sYr = [eso.enter_context(nc.sbuf_tensor(f"sYr{i}", [P, 512], BF16)) for i in range(2)]
            psW = [eso.enter_context(nc.psum_tensor(f"psWo{j}", [P, 512], F32)) for j in range(4)]
            b_lw = c.cnt("l_wf")
            b_lr = c.cnt("l_yr")
            b_gc = c.cnt("g_wc")
            b_tw = c.cnt("t_wo")
            b_vw = c.cnt("v_wo")
            NU = 256

            @block.sync
            def _(s):
                c.fence_wait(s, fence)
                s.wait_ge(c.sem("ca2a"), 1)
                for u in range(NU):
                    tc2, dg, kc = u // 128, (u // 32) % 4, u % 32
                    c.wge(s, "g_wc", b_gc + u - 1)
                    c.dma(s, sWf[u % 2][:, 0:512], dr["wouts"][dg, kc, :, :], "l_wf")
                    c.wge(s, "t_wo", b_tw + u - 1)
                    c.dma(s, sYr[u % 2][:], a2a_out[kc * P:(kc + 1) * P, tc2 * 512:(tc2 + 1) * 512],
                          "l_yr")

            @block.gpsimd
            def _(g):
                c.fence_wait(g, fence)
                for u in range(NU):
                    tc2, dg, kc = u // 128, (u // 32) % 4, u % 32
                    c.wge(g, "g_wc", b_gc + u - 1)
                    c.dma(g, sWf[u % 2][:, 256:512], dr["wouts"][dg, kc, :, 256:512], "l_wp")
                    c.wge(g, "l_wf", b_lw + 16 * (u + 1))
                    c.wge(g, "l_wp", b_lwp + 16 * (u + 1))
                    c.wge(g, "t_wo", b_tw + u - 1)
                    c.inc(g.tensor_copy(out=sWb[u % 2][:, 0:512], in_=sWf[u % 2][:, 0:512]), "g_wc")
                c.inc(g.engine_nop(), "g_ms")

            @block.tensor
            def _(t):
                c.fence_wait(t, fence)
                for u in range(NU):
                    tc2, dg, kc = u // 128, (u // 32) % 4, u % 32
                    grp = u // 32
                    c.wge(t, "g_wc", b_gc + u + 1)
                    c.wge(t, "l_yr", b_lr + 16 * (u + 1))
                    if kc == 0 and grp >= 1:
                        c.wge(t, "v_wo", b_vw + grp)
                    for j in range(4):
                        mm = t.matmul(psW[j][:, :], sWb[u % 2][:, j * P:(j + 1) * P], sYr[u % 2][:],
                                      start=(kc == 0), stop=(kc == 31))
                    c.inc(mm, "t_wo")

            @block.vector
            def _(v):
                c.fence_wait(v, fence)
                for grp in range(8):
                    tc2, dg = grp // 4, grp % 4
                    c.wge(v, "t_wo", b_tw + (grp + 1) * 32)
                    for j in range(4):
                        dc = dg * 4 + j
                        sl = sR[:, dc * TPC + tc2 * 512: dc * TPC + (tc2 + 1) * 512]
                        last = v.scalar_tensor_tensor(out=sl, in0=psW[j][:, :], scalar=1.0, in1=sl,
                                                      op0=ALU.mult, op1=ALU.add)
                    c.inc(last, "v_wo")
                c.inc(v.engine_nop(), "v_ms")
            c.expect("t_wo", b_tw + NU)
            c.expect("v_wo", b_vw + 8)

        if _STOP_PHASE < 10:
            _emit_taps()
            return nc
        fence = c.snap()
        with ExitStack() as esx:
            sXn = esx.enter_context(nc.sbuf_tensor("sXn3", [P, NCH_D * TPC], BF16))
            emit_ln("3", sXn, fence)
            fence = c.snap()
            ffn2_bgs = emit_ffn_a(False, sXn, fence, fence)
        emit_ffn_b(False, ffn2_bgs)

        if _STOP_PHASE < 11:
            _emit_taps()
            return nc
        fence = c.snap()
        with ExitStack() as eso:
            sOut = eso.enter_context(nc.sbuf_tensor("sOut", [P, D], F32))
            psT = eso.enter_context(nc.psum_tensor("psT1", [P, D], F32))
            b_to = c.cnt("t_ot")
            b_vo = c.cnt("v_ot")
            b_go = c.cnt("g_ot")

            @block.tensor
            def _(t):
                c.fence_wait(t, fence)
                for tt in range(8):
                    c.wge(t, "v_ot", b_vo + tt)
                    for ch in range(NCH_D):
                        mm = t.transpose(psT[:, ch * P:(ch + 1) * P],
                                         sR[:, ch * TPC + tt * P: ch * TPC + (tt + 1) * P], sEyeF[:])
                    c.inc(mm, "t_ot")

            @block.vector
            def _(v):
                c.fence_wait(v, fence)
                for tt in range(8):
                    c.wge(v, "t_ot", b_to + tt + 1)
                    c.wge(v, "g_ot", b_go + 16 * tt)
                    c.inc(v.tensor_copy(out=sOut[:], in_=psT[:, :]), "v_ot")

            @block.gpsimd
            def _(g):
                for tt in range(8):
                    c.wge(g, "v_ot", b_vo + tt + 1)
                    c.dma(g, out_t[tt * P:(tt + 1) * P, :], sOut[:], "g_ot")
                c.wge(g, "g_ot", b_go + 16 * 8)
            c.expect("t_ot", b_to + 8)
            c.expect("v_ot", b_vo + 8)
            c.expect("g_ot", b_go + 16 * 8)

        _emit_taps()

    return nc


def _host_inputs(inputs):
    x = np.asarray(inputs["L0_attn_hidden_states"])[0]
    w_in = np.asarray(inputs["L0_attn_W_in"])
    conv_w = np.asarray(inputs["L0_attn_conv_w"])[:, 0, :]
    conv_b = np.asarray(inputs["L0_attn_conv_b"])
    w_x = np.asarray(inputs["L0_attn_W_x"])
    w_dt = np.asarray(inputs["L0_attn_W_dt"])
    b_dt = np.asarray(inputs["L0_attn_b_dt"])
    a_log = np.asarray(inputs["L0_attn_A_log"])
    d_par = np.asarray(inputs["L0_attn_D_param"])
    w_out = np.asarray(inputs["L0_attn_W_out"])
    w1 = np.asarray(inputs["L0_ffn_W1"])
    w2 = np.asarray(inputs["L0_ffn_W2"])
    w3 = np.asarray(inputs["L0_ffn_W3"])

    def stream_oi(wT):
        K, O = wT.shape
        nkc, noc = K // P, O // P
        return np.ascontiguousarray(
            wT.reshape(nkc, P, noc, P).transpose(2, 1, 0, 3).reshape(noc, P, nkc * P))

    w1s = stream_oi(np.ascontiguousarray(w1.T))
    w3s = stream_oi(np.ascontiguousarray(w3.T))
    w2T = np.ascontiguousarray(w2.T)
    w2g = np.ascontiguousarray(w2T.reshape(64, P, D).transpose(1, 0, 2).reshape(HF, D))
    woutT = np.ascontiguousarray(w_out.T)
    wouts = np.ascontiguousarray(woutT.reshape(32, P, 4, 512).transpose(2, 0, 1, 3))

    h64 = _had(64)
    h128 = _had(128).astype(BF)
    bd64 = np.zeros((P, P), np.float32)
    bd64[0:64, 0:64] = h64
    bd64[64:128, 64:128] = h64
    bd64 = bd64.astype(BF)
    eyef = np.eye(P, dtype=np.float32)
    eyeb = eyef.astype(BF)
    onesr = np.ones((1, P), np.float32).astype(BF)
    onesc = np.ones((P, 1), np.float32)
    esel = np.zeros((32, 32 * P), np.float32)
    for n in range(NS):
        esel[n, n * P:(n + 1) * P] = 1.0
        esel[NS + n, (NS + n) * P:(NS + n + 1) * P] = 1.0
    esel = esel.astype(BF)

    winT = np.ascontiguousarray(w_in.T)
    in_maps = []
    for cc in range(NCORES):
        ds = slice(cc * DSC, (cc + 1) * DSC)
        win_slice = np.concatenate(
            [winT[:, cc * DSC:(cc + 1) * DSC], winT[:, DM + cc * DSC: DM + (cc + 1) * DSC]], axis=1)
        wins = np.ascontiguousarray(
            win_slice.reshape(NCH_D, P, 8, P).transpose(2, 1, 0, 3).reshape(8, P, NCH_D * P))
        in_maps.append(dict(
            x_t=np.ascontiguousarray(x[cc * TPC:(cc + 1) * TPC, :]),
            w1s=w1s, w3s=w3s, w2g=w2g, wouts=wouts, wins=wins,
            wxs=np.ascontiguousarray(w_x.T[ds, :].reshape(NDT, P, 160).transpose(1, 0, 2).reshape(P, -1)),
            wdts=np.ascontiguousarray(w_dt.T[:, ds]),
            cws=np.ascontiguousarray(conv_w[ds, :].reshape(NDT, P, 4).transpose(1, 0, 2).reshape(P, -1)),
            cbs=np.ascontiguousarray(conv_b[ds].reshape(NDT, P).T),
            bdts=np.ascontiguousarray(b_dt[ds].reshape(NDT, P).T),
            alogs=np.ascontiguousarray(a_log[ds, :].reshape(NDT, P, NS).transpose(1, 0, 2).reshape(P, -1)),
            dpars=np.ascontiguousarray(d_par[ds].reshape(NDT, P).T),
            h128=h128, bd64=bd64, eyef=eyef, eyeb=eyeb, onesr=onesr, onesc=onesc,
            esel=esel,
        ))
    return in_maps


def kernel(**inputs):
    in_maps = _host_inputs(inputs)
    nc = build_program()
    res = run_bass_kernel_spmd(nc, in_maps, core_ids=list(range(NCORES)))
    out = np.concatenate([res.results[cc]["out_t"] for cc in range(NCORES)], axis=0)
    return out.reshape(1, L, D).astype(np.float32)


if __name__ == "__main__":
    nc = build_program()
    print("built ok;", nc.n_instructions(), "instructions")
